# version 01
# baseline (speedup 1.0000x reference)
import contextlib
import numpy as np
import ml_dtypes

import concourse.bass as bass
import concourse.mybir as mybir
from concourse.bass_utils import run_bass_kernel_spmd

import concourse.bass_utils as _bu
_orig_run_command = _bu.run_command
def _fast_run_command(cmd, **kw):
    import os as _os
    if not _os.environ.get("KM_BIRSIM"):
        cmd = [c.replace("--enable-birsim=true", "--enable-birsim=false")
               if isinstance(c, str) else c for c in cmd]
    return _orig_run_command(cmd, **kw)
_bu.run_command = _fast_run_command

F32 = mybir.dt.float32
BF16 = mybir.dt.bfloat16
I8 = mybir.dt.int8
AF = mybir.ActivationFunctionType
ALU = mybir.AluOpType
AX = mybir.AxisListType

ENGS = ("T", "V", "S", "G", "Y")
FFN_PARTS = 5
ST_PARTS = 9
KM_DUMP = False


class Cfg:
    def __init__(self, nseq=2, tseq=4096):
        self.NSEQ = nseq
        self.TSEQ = tseq
        self.D = 1024
        self.H = 4096
        self.DC = self.D // 128
        self.HC = self.H // 128
        self.CH = 128
        self.TS = 256
        self.CPS = self.TS // self.CH
        self.NSUP = self.TSEQ // self.TS
        self.EPS_LN = 1e-5
        self.EPS_AT = 1e-6


class Builder:
    def __init__(self, nc):
        self.nc = nc
        self.Q = {e: [] for e in ENGS}
        self.sems = {}
        self.cnt = {}
        self.stack = contextlib.ExitStack()
        self.res_free = {}

    def sem(self, name):
        self.sems[name] = self.stack.enter_context(self.nc.semaphore(name))
        self.cnt[name] = 0

    def sb(self, name, shape, dtype=F32):
        return self.stack.enter_context(self.nc.sbuf_tensor(name, shape, dtype))

    def ps(self, name, shape, dtype=F32):
        return self.stack.enter_context(self.nc.psum_tensor(name, shape, dtype))

    def em(self, eng, fn, inc=None, n=1):
        if inc is not None:
            h = self.sems[inc]
            self.cnt[inc] += n
            self.Q[eng].append(lambda e, fn=fn, h=h, n=n: fn(e).then_inc(h, n))
        else:
            self.Q[eng].append(lambda e, fn=fn: fn(e))

    def wt(self, eng, sem, val=None):
        v = self.cnt[sem] if val is None else val
        h = self.sems[sem]
        self.Q[eng].append(lambda e, h=h, v=v: e.wait_ge(h, v))

    def snap(self, sem):
        return self.cnt[sem]

    def flush(self, eng):
        self.wt(eng, eng)

    def claim(self, eng, name):
        ent = self.res_free.get(name)
        if ent is not None:
            for sem, v in ent:
                self.wt(eng, sem, v)

    def release(self, name, *sems):
        self.res_free[name] = [(s, self.cnt[s]) for s in sems]


def build_program(cfg: Cfg, stage: int = 3):
    nc = bass.Bass("TRN2", target_bir_lowering=False, debug=False,
                   detect_race_conditions=False)
    c = cfg
    NSUP_T = c.NSEQ * c.NSUP
    TSH = c.TS + 2

    x_in = nc.dram_tensor("x_in", [c.NSEQ, 128, c.DC, c.TSEQ], F32, kind="ExternalInput").ap()
    y_out = nc.dram_tensor("y_out", [c.NSEQ, 128, c.DC, c.TSEQ], F32, kind="ExternalOutput").ap()
    wq_in = nc.dram_tensor("wq_in", [128, c.DC, c.D], F32, kind="ExternalInput").ap()
    wk_in = nc.dram_tensor("wk_in", [128, c.DC, c.D], F32, kind="ExternalInput").ap()
    wv_in = nc.dram_tensor("wv_in", [128, c.DC, c.D], F32, kind="ExternalInput").ap()
    wo_in = nc.dram_tensor("wo_in", [128, c.DC, c.D], F32, kind="ExternalInput").ap()
    w1_in = nc.dram_tensor("w1_in", [128, c.DC, c.H], F32, kind="ExternalInput").ap()
    w2_in = nc.dram_tensor("w2_in", [128, c.HC, c.D], F32, kind="ExternalInput").ap()
    bq_in = nc.dram_tensor("bq_in", [128, c.DC], F32, kind="ExternalInput").ap()
    bk_in = nc.dram_tensor("bk_in", [128, c.DC], F32, kind="ExternalInput").ap()
    bv_in = nc.dram_tensor("bv_in", [1, c.D], F32, kind="ExternalInput").ap()
    bo_in = nc.dram_tensor("bo_in", [128, c.DC], F32, kind="ExternalInput").ap()
    cw_in = nc.dram_tensor("cw_in", [3, 128, c.HC], F32, kind="ExternalInput").ap()
    cb_in = nc.dram_tensor("cb_in", [128, c.HC], F32, kind="ExternalInput").ap()
    idf_in = nc.dram_tensor("idf_in", [128, 128], F32, kind="ExternalInput").ap()
    idb_in = nc.dram_tensor("idb_in", [128, 128], BF16, kind="ExternalInput").ap()
    mask_in = nc.dram_tensor("mask_in", [128, 128], F32, kind="ExternalInput").ap()
    onesb_in = nc.dram_tensor("onesb_in", [128, 1], BF16, kind="ExternalInput").ap()

    wq_s = nc.dram_tensor("wq_s", [128, c.DC, c.D], BF16, kind="Internal").ap()
    wk_s = nc.dram_tensor("wk_s", [128, c.DC, c.D], BF16, kind="Internal").ap()
    wv_s = nc.dram_tensor("wv_s", [128, c.DC, c.D], BF16, kind="Internal").ap()
    wo_s = nc.dram_tensor("wo_s", [128, c.DC, c.D], BF16, kind="Internal").ap()
    w1q_s = nc.dram_tensor("w1q_s", [c.HC, 128, c.D], BF16, kind="Internal").ap()
    w2q_s = nc.dram_tensor("w2q_s", [c.DC, 128, c.H], BF16, kind="Internal").ap()
    NROWK = 8
    rows_s = nc.dram_tensor("rows_s", [NROWK, 2, NSUP_T, c.TS + 4], F32, kind="Internal").ap()
    dbg_s = nc.dram_tensor("dbg_s", [48, 128], F32, kind="Internal").ap()
    _dk = "ExternalOutput" if KM_DUMP else "Internal"
    dbg2 = nc.dram_tensor("dbg2", [4, 128, 8256], F32, kind=_dk).ap()
    dbgi8 = nc.dram_tensor("dbgi8", [128, 8256], I8, kind=_dk).ap()
    RK_FA, RK_FB, RK_FRS, RK_FS2, RK_FSS2, RK_AA, RK_AB, RK_RD = range(8)

    b = Builder(nc)
    with b.stack:
        for s in ["T", "V", "S", "G"]:
            b.sem(s)
        for i in range(3):
            b.sem(f"sx{i}")
            b.sem(f"so{i}")
            b.sem(f"sw1{i}")
        for i in range(2):
            b.sem(f"sw2{i}")
        b.sem("swa")
        b.sem("srow")
        b.sem("sbc")
        b.sem("scu")
        b.sem("sch")
        b.sem("sck")
        b.sem("spro")

        ring = [b.sb(f"ring{i}", [128, c.DC, TSH], F32) for i in range(3)]
        hr = b.sb("hr", [128, c.HC, TSH], F32)
        hrf = hr[:].rearrange("p a t -> p (a t)")
        uq8 = b.sb("uq8", [128, c.DC, TSH], I8)
        uqb = b.sb("uqb", [128, c.DC, TSH], BF16)
        hq8 = b.sb("hq8", [128, c.HC, c.TS], I8)
        hq8f = hq8[:].rearrange("p a t -> p (a t)")
        hqb = b.sb("hqb", [128, c.HC, c.TS], BF16)
        hqbf = hqb[:].rearrange("p a t -> p (a t)")
        w1sl = [b.sb(f"w1sl{i}", [128, c.DC, 128], BF16) for i in range(2)]
        w2sl = [b.sb(f"w2sl{i}", [128, c.HC, 128], BF16) for i in range(2)]
        wasl = b.sb("wasl", [128, c.DC, 512], BF16)
        kv = b.sb("kv", [128, c.DC, c.D], F32)
        kvb = b.sb("kvb", [128, c.DC, c.D], BF16)
        ks = b.sb("ks", [128, c.DC], F32)
        ksb = b.sb("ksb", [128, c.DC], BF16)
        qf = b.sb("qf", [128, c.DC, c.TS], BF16)
        kf = b.sb("kf", [128, c.DC, c.TS], BF16)
        ktm = b.sb("ktm", [128, c.CPS, c.D], BF16)
        vtm = b.sb("vtm", [128, c.CPS, c.D], BF16)
        uA = b.sb("uA", [128, c.DC, c.TS], BF16)
        afm = b.sb("afm", [128, c.DC, c.TS], BF16)
        AT = b.sb("ATt", [128, 3, 128], BF16)
        bcFA = b.sb("bcFA", [128, TSH], F32)
        bcFB = b.sb("bcFB", [128, TSH], F32)
        bcFRS = b.sb("bcFRS", [128, TSH], F32)
        bcFS2 = b.sb("bcFS2", [128, c.TS], F32)
        bcFSS2 = b.sb("bcFSS2", [128, c.TS], F32)
        bcAA = b.sb("bcAA", [128, c.TS], F32)
        bcAB = b.sb("bcAB", [128, c.TS], F32)
        bcRD = b.sb("bcRD", [128, 128], F32)
        bvbc = b.sb("bvbc", [128, c.D], F32)
        tmpa = b.sb("tmpa", [128, TSH], F32)
        tmpb = b.sb("tmpb", [128, TSH], F32)
        tmpc = b.sb("tmpc", [128, TSH], F32)
        tmpd = b.sb("tmpd", [128, TSH], F32)
        sqdump = hr[:].rearrange("p a t -> p (a t)")
        stat = b.sb("stat", [128, 24], F32)
        rowst3 = b.sb("rowst3", [128, 3, 8], F32)
        rowfl = rowst3[:].rearrange("p a t -> p (a t)")
        kswork = b.sb("kswork", [128, c.DC], F32)
        sliver = [b.sb(f"sliver{i}", [128, c.DC, 1], F32) for i in range(2)]
        convL = b.sb("convL", [128, c.HC], F32)
        convM = b.sb("convM", [128, c.HC], F32)
        convR = b.sb("convR", [128, c.HC], F32)
        convB = b.sb("convB", [128, c.HC], F32)
        batt = {}
        for nm in ["bq", "bk", "bo"]:
            batt[nm] = b.sb(nm, [128, c.DC], F32)
            batt["n" + nm] = b.sb("n" + nm, [128, c.DC], F32)
        idf = b.sb("idf", [128, 128], F32)
        idb = b.sb("idb", [128, 128], BF16)
        maskt = b.sb("maskt", [128, 128], F32)
        onesb = b.sb("onesb", [128, 1], BF16)
        epst = b.sb("epst", [128, 1], F32)
        scal = b.sb("scal", [1, 8], F32)
        scalb = b.sb("scalb", [128, 8], F32)
        den_r = b.sb("den_r", [1, 128], F32)

        P0 = b.ps("P0", [128, 1024], F32)
        P1 = b.ps("P1", [128, 1024], F32)
        P2 = b.ps("P2", [128, TSH], F32)
        P3 = b.ps("P3", [128, TSH], F32)
        P4 = b.ps("P4", [128, TSH], F32)
        PB = b.ps("PBf", [128, 1024], BF16)
        Pm = [P2, P3, P4]
        PmN = ["P2", "P3", "P4"]

        for dst, src in [(idf, idf_in), (idb, idb_in), (maskt, mask_in),
                         (onesb, onesb_in), (convB, cb_in),
                         (batt["bq"], bq_in), (batt["bk"], bk_in), (batt["bo"], bo_in)]:
            b.em("Y", lambda e, d=dst, s=src: e.dma_start(out=d[:], in_=s),
                 inc="spro", n=16)
        for i, t in enumerate([convL, convM, convR]):
            b.em("Y", lambda e, t=t, i=i: e.dma_start(out=t[:], in_=cw_in[i]),
                 inc="spro", n=16)
        b.em("Y", lambda e: e.dma_start(out=bvbc[:], in_=bv_in.partition_broadcast(128)),
             inc="spro", n=16)
        b.wt("V", "spro")
        b.em("V", lambda e: e.memset(epst[:], c.EPS_LN))
        b.em("V", lambda e: e.memset(rowst3[:], 0.0))
        b.em("V", lambda e: e.memset(kswork[:], 0.0))
        b.em("V", lambda e: e.memset(stat[:], 0.0))
        b.em("V", lambda e: e.memset(den_r[:], 0.0))
        b.em("V", lambda e: e.memset(kv[:], 0.0))
        b.em("V", lambda e: e.memset(ks[:], 0.0))
        b.em("V", lambda e: e.memset(ksb[:], 0.0))
        for nm in ["bq", "bk", "bo"]:
            b.em("V", lambda e, s=batt[nm], d=batt["n" + nm]:
                 e.tensor_scalar_mul(out=d[:], in0=s[:], scalar1=-1.0), inc="V")

        QW = 2048
        for wsrc, wdst in [(wq_in, wq_s), (wk_in, wk_s), (wv_in, wv_s), (wo_in, wo_s)]:
            for qtr in range(4):
                sl = slice(qtr * 256, (qtr + 1) * 256)
                b.wt("Y", "V")
                b.em("Y", lambda e, wsrc=wsrc, sl=sl: e.dma_start(
                    out=hrf[:, 0:QW].rearrange("p (a t) -> p a t", a=c.DC),
                    in_=wsrc[:, :, sl]), inc="spro", n=16)
                b.wt("V", "spro")
                b.em("V", lambda e: e.tensor_copy(
                    out=hqbf[:, 0:QW], in_=hrf[:, 0:QW]), inc="V")
                b.wt("Y", "V")
                b.em("Y", lambda e, wdst=wdst, sl=sl: e.dma_start(
                    out=wdst[:, :, sl],
                    in_=hqbf[:, 0:QW].rearrange("p (a t) -> p a t", a=c.DC)),
                    inc="spro", n=16)
        b.wt("V", "spro")

        b.em("V", lambda e: e.memset(tmpa[:, 0:2], 0.0), inc="V")

        _dbg_i = [0]
        def absacc(wsrc, n_m, m_w, col):
            flat_w = m_w * 128
            for m in range(n_m):
                b.wt("Y", "V")
                b.em("Y", lambda e, m=m, wsrc=wsrc, flat_w=flat_w, m_w=m_w:
                     e.dma_start(out=hrf[:, 0:flat_w].rearrange(
                         "p (a t) -> p a t", a=m_w),
                         in_=wsrc[:, :, m * 128:(m + 1) * 128]), inc="spro", n=16)
                b.wt("V", "spro")
                b.em("V", lambda e, flat_w=flat_w: e.tensor_reduce(
                    out=tmpb[:, 0:1], in_=hrf[:, 0:flat_w],
                    axis=AX.X, op=ALU.add, apply_absolute_value=True), inc="V")
                b.flush("V")
                b.em("V", lambda e, col=col: e.tensor_add(
                    out=tmpa[:, col:col + 1], in0=tmpa[:, col:col + 1],
                    in1=tmpb[:, 0:1]), inc="V")
                di = _dbg_i[0]; _dbg_i[0] += 1
                b.wt("G", "V")
                b.em("G", lambda e, di=di: e.dma_start(
                    out=dbg_s[di], in_=tmpb[:, 0:1]), inc="srow", n=16)

        absacc(w1_in, c.HC, c.DC, 0)
        absacc(w2_in, c.DC, c.HC, 1)
        b.wt("T", "V")
        b.claim("T", "P2")
        b.em("T", lambda e: e.transpose(out=P2[0:1, 0:128], in_=tmpa[:, 0:1],
                                        identity=idf[:]), inc="T")
        b.em("T", lambda e: e.transpose(out=P2[0:1, 128:256], in_=tmpa[:, 1:2],
                                        identity=idf[:]), inc="T")
        b.wt("V", "T")
        b.em("V", lambda e: e.tensor_reduce(out=scal[0:1, 2:3], in_=P2[0:1, 0:128],
                                            axis=AX.X, op=ALU.add), inc="V")
        b.em("V", lambda e: e.tensor_reduce(out=scal[0:1, 3:4], in_=P2[0:1, 128:256],
                                            axis=AX.X, op=ALU.add), inc="V")
        b.release("P2", "V")
        b.flush("V")

        def final_scale(col_sum, col_s, n_elems, out_col_m):
            b.em("V", lambda e: e.tensor_scalar(
                out=scal[0:1, col_s:col_s + 1], in0=scal[0:1, col_sum:col_sum + 1],
                scalar1=1.0 / n_elems, scalar2=1e-5, op0=ALU.mult, op1=ALU.max),
                inc="V")
            b.flush("V")
            b.em("V", lambda e: e.tensor_copy(
                out=scal[0:1, out_col_m:out_col_m + 1],
                in_=scal[0:1, col_s:col_s + 1]), inc="V")
            b.flush("V")
            b.em("V", lambda e: e.reciprocal(
                out=scal[0:1, col_s:col_s + 1], in_=scal[0:1, col_s:col_s + 1]),
                inc="V")
            b.flush("V")

        final_scale(2, 0, c.D * c.H, 4)
        final_scale(3, 1, c.D * c.H, 5)
        b.em("V", lambda e: e.tensor_scalar_mul(out=scal[0:1, 6:7], in0=scal[0:1, 4:5],
                                                scalar1=1.0 / 127.0), inc="V")
        b.em("V", lambda e: e.tensor_scalar_mul(out=scal[0:1, 7:8], in0=scal[0:1, 5:6],
                                                scalar1=0.5 / 127.0), inc="V")
        b.flush("V")
        b.wt("G", "V")
        b.em("G", lambda e: e.dma_start(out=rows_s[0, 1, 0, 0:8], in_=scal[0:1, 0:8]),
             inc="srow", n=16)
        b.wt("G", "srow")
        b.em("G", lambda e: e.dma_start(
            out=scalb[:], in_=rows_s[0, 1, 0, 0:8].partition_broadcast(128)),
            inc="sbc", n=16)
        b.wt("V", "sbc")

        def ternarize(wsrc, wdst, n_m, m_w, s_col):
            flat_w = m_w * 128
            for m in range(n_m):
                b.wt("Y", "G")
                b.wt("Y", "V")
                b.wt("Y", "spro")
                b.em("Y", lambda e, m=m, wsrc=wsrc, flat_w=flat_w, m_w=m_w:
                     e.dma_start(out=hrf[:, 0:flat_w].rearrange(
                         "p (a t) -> p a t", a=m_w),
                         in_=wsrc[:, :, m * 128:(m + 1) * 128]), inc="spro", n=16)
                b.wt("V", "spro")
                b.em("V", lambda e, flat_w=flat_w, s_col=s_col: e.tensor_scalar(
                    out=hrf[:, 0:flat_w], in0=hrf[:, 0:flat_w],
                    scalar1=scalb[:, s_col:s_col + 1], scalar2=None,
                    op0=ALU.mult), inc="V")
                b.flush("V")
                b.em("V", lambda e, flat_w=flat_w: e.tensor_scalar(
                    out=hrf[:, 0:flat_w], in0=hrf[:, 0:flat_w],
                    scalar1=-1.0, scalar2=1.0, op0=ALU.max, op1=ALU.min), inc="V")
                b.flush("V")
                b.em("V", lambda e, flat_w=flat_w: e.tensor_copy(
                    out=hq8f[:, 0:flat_w], in_=hrf[:, 0:flat_w]), inc="V")
                b.wt("G", "V")
                b.em("G", lambda e, flat_w=flat_w: e.dma_start(
                    out=hqbf[:, 0:flat_w], in_=hq8f[:, 0:flat_w]), inc="G", n=16)
                b.wt("Y", "G")
                b.em("Y", lambda e, m=m, wdst=wdst, flat_w=flat_w: e.dma_start(
                    out=wdst[m], in_=hqbf[:, 0:flat_w]), inc="spro", n=16)

        ternarize(w1_in, w1q_s, c.HC, c.DC, 0)
        ternarize(w2_in, w2q_s, c.DC, c.HC, 1)
        b.wt("V", "spro")
        b.wt("T", "spro")
        b.wt("G", "spro")

        def stats_rows(seq, sup_i, slot, cols_lo, cols_hi, kinds, is_ffn, sub_idx):
            n_tok = cols_hi - cols_lo
            n_tc = (n_tok + 127) // 128
            src = ring[slot]
            row_i = seq * c.NSUP + sup_i
            b.wt("T", "V")
            b.wt("T", "S")
            for tc in range(n_tc):
                lo = cols_lo + tc * 128
                hi = min(cols_lo + (tc + 1) * 128, cols_hi)
                w = hi - lo
                P = P0 if tc % 2 == 0 else P1
                nmP = "P0" if tc % 2 == 0 else "P1"
                b.claim("T", nmP)
                for fc in range(c.DC):
                    b.em("T", lambda e, P=P, fc=fc, lo=lo, hi=hi, w=w, src=src:
                         e.transpose(out=P[0:w, fc * 128:(fc + 1) * 128],
                                     in_=src[:, fc, lo:hi], identity=idf[:]),
                         inc="T" if fc == c.DC - 1 else None)
                tT = b.snap("T")
                b.wt("V", "T", tT)
                b.wt("S", "T", tT)
                sc = 8 * tc
                if ST_PARTS < 2:
                    b.release(nmP, "V", "S")
                    continue
                b.em("V", lambda e, w=w, sc=sc: e.memset(stat[0:w, sc + 1:sc + 2], 0.0),
                     inc="V")
                b.em("V", lambda e, P=P, w=w, sc=sc: e.tensor_reduce(
                    out=stat[0:w, sc + 0:sc + 1], in_=P[0:w, :], axis=AX.X, op=ALU.add),
                    inc="V")
                if ST_PARTS >= 3:
                    b.wt("S", "V")
                    b.em("S", lambda e, P=P, w=w: e.activation(
                        out=sqdump[0:w, 0:1024], in_=P[0:w, :], func=AF.Square),
                        inc="S")
                    b.wt("V", "S")
                    b.em("V", lambda e, w=w, sc=sc: e.tensor_reduce(
                        out=stat[0:w, sc + 1:sc + 2], in_=sqdump[0:w, 0:1024],
                        axis=AX.X, op=ALU.add), inc="V")
                if is_ffn:
                    b.em("V", lambda e, P=P, w=w, sc=sc: e.tensor_reduce(
                        out=stat[0:w, sc + 2:sc + 3], in_=P[0:w, :], axis=AX.X,
                        op=ALU.max), inc="V")
                    b.em("V", lambda e, P=P, w=w, sc=sc: e.tensor_reduce(
                        out=stat[0:w, sc + 3:sc + 4], in_=P[0:w, :], axis=AX.X,
                        op=ALU.min), inc="V")
                b.release(nmP, "V", "S")
            if ST_PARTS < 4:
                return
            b.wt("V", "S")
            for tc in range(n_tc):
                lo = cols_lo + tc * 128
                hi = min(cols_lo + (tc + 1) * 128, cols_hi)
                w = hi - lo
                sc = 8 * tc
                b.claim("V", f"rowst{tc}")
                b.em("V", lambda e, w=w, sc=sc: e.tensor_scalar_mul(
                    out=stat[0:w, sc + 4:sc + 5], in0=stat[0:w, sc + 0:sc + 1],
                    scalar1=1.0 / c.D), inc="V")
                b.flush("V")
                b.em("V", lambda e, tc=tc, w=w, sc=sc: e.tensor_mul(
                    out=rowst3[0:w, tc, 0:1], in0=stat[0:w, sc + 4:sc + 5],
                    in1=stat[0:w, sc + 4:sc + 5]), inc="V")
                b.flush("V")
                b.em("V", lambda e, tc=tc, w=w, sc=sc: e.scalar_tensor_tensor(
                    out=stat[0:w, sc + 5:sc + 6], in0=stat[0:w, sc + 1:sc + 2],
                    scalar=1.0 / c.D, in1=rowst3[0:w, tc, 0:1],
                    op0=ALU.mult, op1=ALU.subtract), inc="V")
                b.flush("V")
                b.wt("S", "V")
                b.em("S", lambda e, w=w, sc=sc: e.activation(
                    out=stat[0:w, sc + 5:sc + 6], in_=stat[0:w, sc + 5:sc + 6],
                    func=AF.Sqrt, bias=epst[0:w, :], scale=1.0), inc="S")
                b.wt("V", "S")
                b.em("V", lambda e, w=w, sc=sc: e.reciprocal(
                    out=stat[0:w, sc + 5:sc + 6], in_=stat[0:w, sc + 5:sc + 6]),
                    inc="V")
                b.flush("V")
                if is_ffn:
                    b.em("V", lambda e, tc=tc, w=w, sc=sc: e.tensor_sub(
                        out=rowst3[0:w, tc, 1:2], in0=stat[0:w, sc + 2:sc + 3],
                        in1=stat[0:w, sc + 4:sc + 5]), inc="V")
                    b.flush("V")
                    b.em("V", lambda e, tc=tc, w=w, sc=sc: e.tensor_sub(
                        out=rowst3[0:w, tc, 2:3], in0=stat[0:w, sc + 4:sc + 5],
                        in1=stat[0:w, sc + 3:sc + 4]), inc="V")
                    b.flush("V")
                    b.em("V", lambda e, tc=tc, w=w: e.tensor_max(
                        out=rowst3[0:w, tc, 1:2], in0=rowst3[0:w, tc, 1:2],
                        in1=rowst3[0:w, tc, 2:3]), inc="V")
                    b.flush("V")
                    b.em("V", lambda e, tc=tc, w=w, sc=sc: e.tensor_mul(
                        out=rowst3[0:w, tc, 1:2], in0=rowst3[0:w, tc, 1:2],
                        in1=stat[0:w, sc + 5:sc + 6]), inc="V")
                    b.flush("V")
                    b.em("V", lambda e, tc=tc, w=w: e.tensor_scalar_max(
                        out=rowst3[0:w, tc, 1:2], in0=rowst3[0:w, tc, 1:2], scalar1=1e-5),
                        inc="V")
                    b.flush("V")
                    b.em("V", lambda e, tc=tc, w=w: e.reciprocal(
                        out=rowst3[0:w, tc, 3:4], in_=rowst3[0:w, tc, 1:2]), inc="V")
                    b.flush("V")
                    b.em("V", lambda e, tc=tc, w=w: e.tensor_scalar_mul(
                        out=rowst3[0:w, tc, 3:4], in0=rowst3[0:w, tc, 3:4], scalar1=127.0),
                        inc="V")
                    b.flush("V")
                    b.em("V", lambda e, tc=tc, w=w: e.tensor_scalar(
                        out=rowst3[0:w, tc, 4:5], in0=rowst3[0:w, tc, 1:2],
                        scalar1=scalb[0:w, 6:7], scalar2=None, op0=ALU.mult), inc="V")
                    b.flush("V")
                    b.em("V", lambda e, tc=tc, w=w, sc=sc: e.tensor_mul(
                        out=rowst3[0:w, tc, 5:6], in0=stat[0:w, sc + 5:sc + 6],
                        in1=rowst3[0:w, tc, 3:4]), inc="V")
                    b.flush("V")
                    b.em("V", lambda e, tc=tc, w=w, sc=sc: e.scalar_tensor_tensor(
                        out=rowst3[0:w, tc, 6:7], in0=stat[0:w, sc + 4:sc + 5],
                        scalar=-1.0, in1=rowst3[0:w, tc, 5:6],
                        op0=ALU.mult, op1=ALU.mult), inc="V")
                    b.flush("V")
                    emitted = [(RK_FA, 5), (RK_FB, 6), (RK_FRS, 4)]
                else:
                    b.em("V", lambda e, tc=tc, w=w, sc=sc: e.tensor_copy(
                        out=rowst3[0:w, tc, 5:6], in_=stat[0:w, sc + 5:sc + 6]), inc="V")
                    b.flush("V")
                    b.em("V", lambda e, tc=tc, w=w, sc=sc: e.scalar_tensor_tensor(
                        out=rowst3[0:w, tc, 6:7], in0=stat[0:w, sc + 4:sc + 5],
                        scalar=-1.0, in1=rowst3[0:w, tc, 5:6],
                        op0=ALU.mult, op1=ALU.mult), inc="V")
                    b.flush("V")
                    emitted = [(RK_AA, 5), (RK_AB, 6)]
                if ST_PARTS < 5:
                    continue
                b.wt("G", "V")
                for rk, colj in emitted:
                    b.em("G", lambda e, rk=rk, colj=colj, w=w, tc=tc, row_i=row_i,
                         sub_idx=sub_idx: e.dma_start(
                        out=rows_s[rk, sub_idx, row_i, tc * 128:tc * 128 + w],
                        in_=rowfl[0:w, tc * 8 + colj:tc * 8 + colj + 1]),
                        inc="srow", n=16)
                b.release(f"rowst{tc}", "srow")
            b.wt("G", "srow")
            for rk, tile, wdt in kinds:
                b.em("G", lambda e, rk=rk, tile=tile, wdt=wdt, row_i=row_i,
                     sub_idx=sub_idx: e.dma_start(
                    out=tile[:, 0:wdt],
                    in_=rows_s[rk, sub_idx, row_i, 0:wdt].partition_broadcast(128)),
                    inc="sbc", n=16)

        def ffn(seq, sup_i, slot, sub_idx, first_sup, last_sup):
            if ST_PARTS < 9 and FFN_PARTS < 1:
                stats_rows(seq, sup_i, slot, 0, TSH,
                           [(RK_FA, bcFA, TSH), (RK_FB, bcFB, TSH),
                            (RK_FRS, bcFRS, TSH)], True, sub_idx)
                return
            stats_rows(seq, sup_i, slot, 0, TSH,
                       [(RK_FA, bcFA, TSH), (RK_FB, bcFB, TSH), (RK_FRS, bcFRS, TSH)],
                       True, sub_idx)
            if FFN_PARTS < 1:
                return
            vbc = b.snap("sbc")
            b.wt("V", "sbc", vbc)
            tmpU = hr[:].rearrange("p a t -> p (a t)")[:, 0:c.DC * TSH] \
                .rearrange("p (a t) -> p a t", a=c.DC)
            b.em("V", lambda e, slot=slot, tmpU=tmpU: e.tensor_tensor(
                out=tmpU, in0=ring[slot][:],
                in1=bcFA[:, 0:TSH].unsqueeze(1).broadcast_to([128, c.DC, TSH]),
                op=ALU.mult), inc="V")
            b.flush("V")
            b.em("V", lambda e, tmpU=tmpU: e.tensor_tensor(
                out=uq8[:], in0=tmpU,
                in1=bcFB[:, 0:TSH].unsqueeze(1).broadcast_to([128, c.DC, TSH]),
                op=ALU.add), inc="V")
            if first_sup:
                b.em("V", lambda e: e.memset(uq8[:, :, 0:1], 0), inc="V")
            if last_sup:
                b.em("V", lambda e: e.memset(uq8[:, :, TSH - 1:TSH], 0), inc="V")
            b.wt("G", "V")
            b.em("G", lambda e: e.dma_start(out=uqb[:], in_=uq8[:]), inc="scu", n=16)
            scu_t = b.snap("scu")
            if FFN_PARTS < 2:
                return
            for m in range(c.HC):
                wsl = w1sl[m % 2]
                wsem = f"sw1{m % 2}"
                b.claim("Y", f"w1c{m % 2}")
                b.em("Y", lambda e, m=m, wsl=wsl: e.dma_start(
                    out=wsl[:].rearrange("p a t -> p (a t)"), in_=w1q_s[m]),
                    inc=wsem, n=16)
                Pn = PmN[m % 3]
                P = Pm[m % 3]
                b.claim("T", Pn)
                b.wt("T", wsem)
                if m == 0:
                    b.wt("T", "scu", scu_t)
                for k in range(c.DC):
                    b.em("T", lambda e, P=P, k=k, wsl=wsl: e.matmul(
                        out=P[:, 0:TSH], lhsT=wsl[:, k, :], rhs=uqb[:, k, :],
                        start=(k == 0), stop=(k == c.DC - 1)),
                        inc="T" if k == c.DC - 1 else None)
                b.release(f"w1c{m % 2}", "T")
                b.wt("V", "T")
                if m == 0:
                    b.wt("V", "sbc", vbc)
                b.em("V", lambda e, P=P, m=m: e.tensor_mul(
                    out=hr[:, m, :], in0=P[:, 0:TSH], in1=bcFRS[:, 0:TSH]), inc="V")
                b.release(Pn, "V")
                if FFN_PARTS < 3:
                    continue
                b.wt("S", "V")
                ct = tmpb if m % 2 == 0 else tmpc
                b.em("S", lambda e, m=m, ct=ct: e.activation(
                    out=ct[:, 0:c.TS], in_=hr[:, m, 1:TSH - 1], func=AF.Identity,
                    bias=convB[:, m:m + 1], scale=convM[:, m:m + 1]), inc="S")
                b.wt("V", "S")
                b.em("V", lambda e, m=m, ct=ct: e.scalar_tensor_tensor(
                    out=ct[:, 0:c.TS], in0=hr[:, m, 0:TSH - 2],
                    scalar=convL[:, m:m + 1], in1=ct[:, 0:c.TS],
                    op0=ALU.mult, op1=ALU.add), inc="V")
                b.flush("V")
                b.em("V", lambda e, m=m, ct=ct: e.scalar_tensor_tensor(
                    out=ct[:, 0:c.TS], in0=hr[:, m, 2:TSH],
                    scalar=convR[:, m:m + 1], in1=ct[:, 0:c.TS],
                    op0=ALU.mult, op1=ALU.add), inc="V")
                b.wt("S", "V")
                b.em("S", lambda e, m=m, ct=ct: e.activation(
                    out=hr[:, m, 0:c.TS], in_=ct[:, 0:c.TS], func=AF.Silu), inc="S")
            if FFN_PARTS < 4:
                return
            b.wt("V", "S")
            b.em("V", lambda e: e.tensor_reduce(
                out=tmpa[:, 0:c.TS], in_=hr[:, :, 0:c.TS].transpose([0, 2, 1]),
                axis=AX.X, op=ALU.max, apply_absolute_value=True), inc="V")
            b.wt("T", "V")
            b.claim("T", "P0")
            for tc in range(c.CPS):
                b.em("T", lambda e, tc=tc: e.transpose(
                    out=P0[:, tc * 128:(tc + 1) * 128],
                    in_=tmpa[:, tc * 128:(tc + 1) * 128], identity=idf[:]),
                    inc="T" if tc == c.CPS - 1 else None)
            b.wt("V", "T")
            row_i = seq * c.NSUP + sup_i
            for tc in range(c.CPS):
                b.claim("V", f"rowst{tc}")
                b.em("V", lambda e, tc=tc: e.tensor_reduce(
                    out=rowst3[:, tc, 0:1], in_=P0[:, tc * 128:(tc + 1) * 128],
                    axis=AX.X, op=ALU.max), inc="V")
                b.flush("V")
                b.em("V", lambda e, tc=tc: e.tensor_scalar_max(
                    out=rowst3[:, tc, 0:1], in0=rowst3[:, tc, 0:1], scalar1=1e-5), inc="V")
                b.flush("V")
                b.em("V", lambda e, tc=tc: e.reciprocal(out=rowst3[:, tc, 1:2], in_=rowst3[:, tc, 0:1]),
                     inc="V")
                b.flush("V")
                b.em("V", lambda e, tc=tc: e.tensor_scalar_mul(
                    out=rowst3[:, tc, 1:2], in0=rowst3[:, tc, 1:2], scalar1=127.0), inc="V")
                b.flush("V")
                b.em("V", lambda e, tc=tc: e.tensor_scalar(
                    out=rowst3[:, tc, 2:3], in0=rowst3[:, tc, 0:1],
                    scalar1=scalb[:, 7:8], scalar2=None, op0=ALU.mult), inc="V")
                b.flush("V")
                b.wt("G", "V")
                b.em("G", lambda e, tc=tc, row_i=row_i, sub_idx=sub_idx: e.dma_start(
                    out=rows_s[RK_FS2, sub_idx, row_i, tc * 128:(tc + 1) * 128],
                    in_=rowfl[:, tc * 8 + 1:tc * 8 + 2]), inc="srow", n=16)
                b.em("G", lambda e, tc=tc, row_i=row_i, sub_idx=sub_idx: e.dma_start(
                    out=rows_s[RK_FSS2, sub_idx, row_i, tc * 128:(tc + 1) * 128],
                    in_=rowfl[:, tc * 8 + 2:tc * 8 + 3]), inc="srow", n=16)
                b.release(f"rowst{tc}", "srow")
            b.release("P0", "V")
            b.wt("G", "srow")
            b.em("G", lambda e, row_i=row_i, sub_idx=sub_idx: e.dma_start(
                out=bcFS2[:, 0:c.TS],
                in_=rows_s[RK_FS2, sub_idx, row_i, 0:c.TS].partition_broadcast(128)),
                inc="sbc", n=16)
            b.em("G", lambda e, row_i=row_i, sub_idx=sub_idx: e.dma_start(
                out=bcFSS2[:, 0:c.TS],
                in_=rows_s[RK_FSS2, sub_idx, row_i, 0:c.TS].partition_broadcast(128)),
                inc="sbc", n=16)
            b.wt("V", "sbc")
            b.em("V", lambda e: e.tensor_tensor(
                out=hq8[:], in0=hr[:, :, 0:c.TS],
                in1=bcFS2[:, 0:c.TS].unsqueeze(1).broadcast_to([128, c.HC, c.TS]),
                op=ALU.mult), inc="V")
            b.wt("G", "V")
            b.em("G", lambda e: e.dma_start(out=hqb[:], in_=hq8[:]), inc="sch", n=16)
            sch_t = b.snap("sch")
            if FFN_PARTS < 5:
                return
            for mg in range(2):
                Pg = P0 if mg == 0 else P1
                nmP = "P0" if mg == 0 else "P1"
                b.claim("T", nmP)
                if mg == 0:
                    b.wt("T", "sch", sch_t)
                for mi in range(4):
                    m = mg * 4 + mi
                    wsl = w2sl[m % 2]
                    wsem = f"sw2{m % 2}"
                    b.claim("Y", f"w2c{m % 2}")
                    b.em("Y", lambda e, m=m, wsl=wsl: e.dma_start(
                        out=wsl[:].rearrange("p a t -> p (a t)"), in_=w2q_s[m]),
                        inc=wsem, n=16)
                    b.wt("T", wsem)
                    for k in range(c.HC):
                        b.em("T", lambda e, Pg=Pg, mi=mi, k=k, wsl=wsl: e.matmul(
                            out=Pg[:, mi * 256:mi * 256 + c.TS],
                            lhsT=wsl[:, k, :], rhs=hqb[:, k, :],
                            start=(k == 0), stop=(k == c.HC - 1)),
                            inc="T" if k == c.HC - 1 else None)
                    b.release(f"w2c{m % 2}", "T")
                b.wt("V", "T")
                b.wt("V", "sbc")
                tmpW = hr[:].rearrange("p a t -> p (a t)")[:, 0:4 * c.TS] \
                    .rearrange("p (a t) -> p a t", a=4)
                for mi in range(4):
                    b.em("V", lambda e, Pg=Pg, mi=mi, tmpW=tmpW: e.tensor_mul(
                        out=tmpW[:, mi, :], in0=Pg[:, mi * 256:mi * 256 + c.TS],
                        in1=bcFSS2[:, 0:c.TS]), inc="V")
                b.release(nmP, "V")
                b.flush("V")
                b.em("V", lambda e, mg=mg, slot=slot, tmpW=tmpW: e.tensor_tensor(
                    out=ring[slot][:, mg * 4:(mg + 1) * 4, 1:TSH - 1],
                    in0=ring[slot][:, mg * 4:(mg + 1) * 4, 1:TSH - 1],
                    in1=tmpW, op=ALU.add), inc="V")
            if KM_DUMP and seq == 0 and sup_i == 0 and sub_idx == 0:
                b.wt("Y", "V")
                b.wt("Y", "S")
                b.em("Y", lambda e: e.dma_start(out=dbg2[0, :, 0:TSH], in_=bcFRS[:]),
                     inc="spro", n=16)
                b.em("Y", lambda e: e.dma_start(out=dbg2[1, :, 0:c.TS], in_=bcFS2[:]),
                     inc="spro", n=16)
                b.em("Y", lambda e: e.dma_start(out=dbg2[1, :, 300:308], in_=scalb[:]),
                     inc="spro", n=16)
                b.em("Y", lambda e: e.dma_start(out=dbg2[0, 0:1, 300:308], in_=scal[:]),
                     inc="spro", n=16)
                b.em("Y", lambda e: e.dma_start(out=dbg2[2, :, 0:c.TS], in_=bcFSS2[:]),
                     inc="spro", n=16)
                b.em("Y", lambda e: e.dma_start(
                    out=dbg2[3], in_=hr[:].rearrange("p a t -> p (a t)")),
                    inc="spro", n=16)
                b.em("Y", lambda e: e.dma_start(
                    out=dbgi8[:, 0:c.HC * c.TS],
                    in_=hq8[:].rearrange("p a t -> p (a t)")), inc="spro", n=16)
                b.wt("Y", "spro")

        def attn(seq, sup_i, slot):
            row_i = seq * c.NSUP + sup_i
            stats_rows(seq, sup_i, slot, 1, TSH - 1,
                       [(RK_AA, bcAA, c.TS), (RK_AB, bcAB, c.TS)], False, 0)
            b.wt("V", "sbc")
            tmpU = hr[:].rearrange("p a t -> p (a t)")[:, 0:c.DC * c.TS] \
                .rearrange("p (a t) -> p a t", a=c.DC)
            b.em("V", lambda e, slot=slot, tmpU=tmpU: e.tensor_tensor(
                out=tmpU, in0=ring[slot][:, :, 1:TSH - 1],
                in1=bcAA[:, 0:c.TS].unsqueeze(1).broadcast_to([128, c.DC, c.TS]),
                op=ALU.mult), inc="V")
            b.flush("V")
            b.em("V", lambda e, tmpU=tmpU: e.tensor_tensor(
                out=uA[:], in0=tmpU,
                in1=bcAB[:, 0:c.TS].unsqueeze(1).broadcast_to([128, c.DC, c.TS]),
                op=ALU.add), inc="V")
            vuA = b.snap("V")
            for (wsrc, dst, bt, nbt) in [(wq_s, qf, batt["bq"], batt["nbq"]),
                                         (wk_s, kf, batt["bk"], batt["nbk"])]:
                for hf in range(2):
                    b.wt("Y", "T")
                    b.em("Y", lambda e, wsrc=wsrc, hf=hf: e.dma_start(
                        out=wasl[:], in_=wsrc[:, :, hf * 512:(hf + 1) * 512]),
                        inc="swa", n=16)
                    b.wt("T", "swa")
                    b.wt("T", "V", vuA)
                    for ml in range(4):
                        m = hf * 4 + ml
                        Pn = "P2" if m % 2 == 0 else "P3"
                        P = P2 if m % 2 == 0 else P3
                        b.claim("T", Pn)
                        for k in range(c.DC):
                            b.em("T", lambda e, P=P, k=k, ml=ml: e.matmul(
                                out=P[:, 0:c.TS],
                                lhsT=wasl[:, k, ml * 128:(ml + 1) * 128],
                                rhs=uA[:, k, :], start=(k == 0),
                                stop=(k == c.DC - 1)),
                                inc="T" if k == c.DC - 1 else None)
                        rt = tmpb if m % 2 == 0 else tmpa
                        et = tmpc if m % 2 == 0 else tmpd
                        b.wt("S", "T")
                        b.em("S", lambda e, P=P, m=m, bt=bt, rt=rt: e.activation(
                            out=rt[:, 0:c.TS], in_=P[:, 0:c.TS], func=AF.Relu,
                            bias=bt[:, m:m + 1], scale=1.0), inc="S")
                        b.em("S", lambda e, P=P, m=m, nbt=nbt, et=et: e.activation(
                            out=et[:, 0:c.TS], in_=P[:, 0:c.TS], func=AF.Relu,
                            bias=nbt[:, m:m + 1], scale=-1.0), inc="S")
                        b.release(Pn, "S")
                        b.wt("V", "S")
                        b.em("V", lambda e, et=et: e.tensor_scalar_mul(
                            out=et[:, 0:c.TS], in0=et[:, 0:c.TS], scalar1=-1.0),
                            inc="V")
                        b.wt("S", "V")
                        b.em("S", lambda e, et=et: e.activation(
                            out=et[:, 0:c.TS], in_=et[:, 0:c.TS], func=AF.Exp,
                            scale=1.0), inc="S")
                        b.wt("V", "S")
                        b.em("V", lambda e, dst=dst, m=m, rt=rt, et=et: e.tensor_add(
                            out=dst[:, m, :], in0=rt[:, 0:c.TS],
                            in1=et[:, 0:c.TS]), inc="V")
            for nh in range(2):
                b.wt("Y", "T")
                b.em("Y", lambda e, nh=nh: e.dma_start(
                    out=wasl[:], in_=wv_s[:, :, nh * 512:(nh + 1) * 512]),
                    inc="swa", n=16)
                b.wt("T", "swa")
                b.wt("T", "V", vuA)
                for tc in range(c.CPS):
                    b.claim("T", "P0")
                    for k in range(c.DC):
                        b.em("T", lambda e, tc=tc, k=k: e.matmul(
                            out=P0[:, 0:512],
                            lhsT=uA[:, k, tc * 128:(tc + 1) * 128],
                            rhs=wasl[:, k, :], start=(k == 0),
                            stop=(k == c.DC - 1)),
                            inc="T" if k == c.DC - 1 else None)
                    b.wt("V", "T")
                    b.em("V", lambda e, tc=tc, nh=nh: e.tensor_add(
                        out=vtm[:, tc, nh * 512:(nh + 1) * 512], in0=P0[:, 0:512],
                        in1=bvbc[:, nh * 512:(nh + 1) * 512]), inc="V")
                    b.release("P0", "V")
            b.wt("T", "V")
            b.claim("T", "PBf")
            for tc in range(c.CPS):
                for fc in range(c.DC):
                    b.em("T", lambda e, tc=tc, fc=fc: e.transpose(
                        out=PB[:, fc * 128:(fc + 1) * 128],
                        in_=kf[:, fc, tc * 128:(tc + 1) * 128], identity=idb[:]),
                        inc="T" if fc == c.DC - 1 else None)
                b.wt("S", "T")
                b.em("S", lambda e, tc=tc: e.copy(out=ktm[:, tc, :], in_=PB[:, 0:c.D]),
                     inc="S")
                if tc < c.CPS - 1:
                    b.wt("T", "S")
            b.release("PBf", "S")
            b.claim("G", "kvbfree")
            b.em("G", lambda e: e.dma_start(out=kvb[:], in_=kv[:]), inc="sck", n=16)
            sck_t = b.snap("sck")
            ai_base = 0
            for ch in range(c.CPS):
                ccols = slice(ch * 128, (ch + 1) * 128)
                at_idx = {}
                for sb in range(ch + 1):
                    ai = ai_base + sb
                    at_idx[sb] = ai
                    b.claim("T", "P4")
                    for k in range(c.DC):
                        b.em("T", lambda e, sb=sb, k=k, ccols=ccols: e.matmul(
                            out=P4[:, 0:128], lhsT=kf[:, k, sb * 128:(sb + 1) * 128],
                            rhs=qf[:, k, ccols], start=(k == 0),
                            stop=(k == c.DC - 1)),
                            inc="T" if k == c.DC - 1 else None)
                    if sb == ch:
                        b.wt("V", "T")
                        b.em("V", lambda e, ai=ai: e.tensor_mul(
                            out=AT[:, ai, :], in0=P4[:, 0:128], in1=maskt[:]),
                            inc="V")
                        b.release("P4", "V")
                    else:
                        b.wt("S", "T")
                        b.em("S", lambda e, ai=ai: e.copy(
                            out=AT[:, ai, :], in_=P4[:, 0:128]), inc="S")
                        b.release("P4", "S")
                ai_base += ch + 1
                b.wt("T", "V")
                b.wt("T", "S")
                b.claim("T", "P2")
                nmm = (ch + 1) + c.DC
                i_mm = 0
                for sb in range(ch + 1):
                    i_mm += 1
                    b.em("T", lambda e, sb=sb, at_idx=at_idx, i_mm=i_mm, nmm=nmm:
                         e.matmul(out=P2[0:1, 0:128], lhsT=onesb[:],
                                  rhs=AT[:, at_idx[sb], :],
                                  start=(i_mm == 1), stop=(i_mm == nmm)))
                for k in range(c.DC):
                    i_mm += 1
                    b.em("T", lambda e, k=k, ccols=ccols, i_mm=i_mm, nmm=nmm:
                         e.matmul(out=P2[0:1, 0:128], lhsT=ksb[:, k:k + 1],
                                  rhs=qf[:, k, ccols],
                                  start=(i_mm == 1), stop=(i_mm == nmm)),
                         inc="T" if i_mm == nmm else None)
                b.wt("V", "T")
                b.em("V", lambda e: e.tensor_scalar_add(
                    out=den_r[:, :], in0=P2[0:1, 0:128], scalar1=c.EPS_AT), inc="V")
                b.flush("V")
                b.em("V", lambda e: e.reciprocal(out=den_r[:, :], in_=den_r[:, :]),
                     inc="V")
                b.flush("V")
                b.release("P2", "V")
                b.wt("G", "V")
                b.em("G", lambda e, row_i=row_i, ch=ch: e.dma_start(
                    out=rows_s[RK_RD, 0, row_i, ch * 128:(ch + 1) * 128],
                    in_=den_r[0:1, :]), inc="srow", n=16)
                b.wt("G", "srow")
                b.em("G", lambda e, row_i=row_i, ch=ch: e.dma_start(
                    out=bcRD[:, :],
                    in_=rows_s[RK_RD, 0, row_i, ch * 128:(ch + 1) * 128]
                        .partition_broadcast(128)), inc="sbc", n=16)
                b.claim("T", "P1")
                b.wt("T", "sck", sck_t)
                for m in range(c.DC):
                    n_steps = (ch + 1) + c.DC
                    step = 0
                    for sb in range(ch + 1):
                        step += 1
                        b.em("T", lambda e, sb=sb, m=m, at_idx=at_idx, step=step,
                             n_steps=n_steps: e.matmul(
                            out=P1[:, m * 128:(m + 1) * 128],
                            lhsT=vtm[:, sb, m * 128:(m + 1) * 128],
                            rhs=AT[:, at_idx[sb], :],
                            start=(step == 1), stop=(step == n_steps)))
                    for d in range(c.DC):
                        step += 1
                        b.em("T", lambda e, d=d, m=m, ccols=ccols, step=step,
                             n_steps=n_steps: e.matmul(
                            out=P1[:, m * 128:(m + 1) * 128],
                            lhsT=kvb[:, d, m * 128:(m + 1) * 128],
                            rhs=qf[:, d, ccols],
                            start=(step == 1), stop=(step == n_steps)),
                            inc="T" if (step == n_steps and m == c.DC - 1) else None)
                b.wt("V", "T")
                b.wt("V", "sbc")
                for m in range(c.DC):
                    b.em("V", lambda e, m=m, ch=ch: e.tensor_mul(
                        out=afm[:, m, ch * 128:(ch + 1) * 128],
                        in0=P1[:, m * 128:(m + 1) * 128], in1=bcRD[:, :]), inc="V")
                b.release("P1", "V")
            b.release("kvbfree", "T")
            for d in range(c.DC):
                Pn = "P0" if d % 2 == 0 else "P1"
                P = P0 if d % 2 == 0 else P1
                b.claim("T", Pn)
                nmm = c.CPS * 2
                i_mm = 0
                for tc in range(c.CPS):
                    for nh in range(2):
                        i_mm += 1
                        b.em("T", lambda e, P=P, d=d, tc=tc, nh=nh, i_mm=i_mm,
                             nmm=nmm: e.matmul(
                            out=P[:, nh * 512:(nh + 1) * 512],
                            lhsT=ktm[:, tc, d * 128:(d + 1) * 128],
                            rhs=vtm[:, tc, nh * 512:(nh + 1) * 512],
                            start=(tc == 0), stop=(tc == c.CPS - 1)),
                            inc="T" if i_mm == nmm else None)
                b.wt("V", "T")
                b.em("V", lambda e, P=P, d=d: e.tensor_add(
                    out=kv[:, d, :], in0=kv[:, d, :], in1=P[:, 0:c.D]), inc="V")
                b.release(Pn, "V")
            b.em("V", lambda e: e.tensor_reduce(
                out=kswork[:], in_=kf[:], axis=AX.X, op=ALU.add), inc="V")
            b.flush("V")
            b.em("V", lambda e: e.tensor_add(out=ks[:], in0=ks[:],
                                             in1=kswork[:]), inc="V")
            b.flush("V")
            b.wt("S", "V")
            b.em("S", lambda e: e.copy(out=ksb[:], in_=ks[:]), inc="S")
            b.wt("T", "V")
            for hf in range(2):
                b.wt("Y", "T")
                b.em("Y", lambda e, hf=hf: e.dma_start(
                    out=wasl[:], in_=wo_s[:, :, hf * 512:(hf + 1) * 512]),
                    inc="swa", n=16)
                b.wt("T", "swa")
                for ml in range(4):
                    m = hf * 4 + ml
                    Pn = "P2" if m % 2 == 0 else "P3"
                    P = P2 if m % 2 == 0 else P3
                    b.claim("T", Pn)
                    for k in range(c.DC):
                        b.em("T", lambda e, P=P, k=k, ml=ml: e.matmul(
                            out=P[:, 0:c.TS],
                            lhsT=wasl[:, k, ml * 128:(ml + 1) * 128],
                            rhs=afm[:, k, :], start=(k == 0),
                            stop=(k == c.DC - 1)),
                            inc="T" if k == c.DC - 1 else None)
                    b.wt("V", "T")
                    b.em("V", lambda e, P=P, m=m, slot=slot: e.scalar_tensor_tensor(
                        out=ring[slot][:, m, 1:TSH - 1], in0=P[:, 0:c.TS],
                        scalar=batt["bo"][:, m:m + 1],
                        in1=ring[slot][:, m, 1:TSH - 1], op0=ALU.add, op1=ALU.add),
                        inc="V")
                    b.release(Pn, "V")

        def gslot(seq, si):
            return (seq * c.NSUP + si) % 3

        def load_x(seq, si):
            slot = gslot(seq, si)
            sem = f"sx{slot}"
            b.claim("Y", f"slotfree{slot}")
            if c.NSUP == 1:
                b.em("Y", lambda e, seq=seq, slot=slot: e.dma_start(
                    out=ring[slot][:, :, 1:TSH - 1],
                    in_=x_in[seq, :, :, 0:c.TS]), inc=sem, n=16)
                b.wt("V", sem)
                b.em("V", lambda e, slot=slot: e.memset(ring[slot][:, :, 0:1], 0.0),
                     inc="V")
                b.em("V", lambda e, slot=slot: e.memset(
                    ring[slot][:, :, TSH - 1:TSH], 0.0), inc="V")
            elif si == 0:
                b.em("Y", lambda e, seq=seq, slot=slot: e.dma_start(
                    out=ring[slot][:, :, 1:TSH],
                    in_=x_in[seq, :, :, 0:c.TS + 1]), inc=sem, n=16)
                b.wt("V", sem)
                b.em("V", lambda e, slot=slot: e.memset(ring[slot][:, :, 0:1], 0.0),
                     inc="V")
            elif si == c.NSUP - 1:
                b.em("Y", lambda e, si=si, seq=seq, slot=slot: e.dma_start(
                    out=ring[slot][:, :, 0:TSH - 1],
                    in_=x_in[seq, :, :, si * c.TS - 1:c.TSEQ]), inc=sem, n=16)
                b.wt("V", sem)
                b.em("V", lambda e, slot=slot: e.memset(
                    ring[slot][:, :, TSH - 1:TSH], 0.0), inc="V")
            else:
                b.em("Y", lambda e, seq=seq, si=si, slot=slot: e.dma_start(
                    out=ring[slot][:],
                    in_=x_in[seq, :, :, si * c.TS - 1:(si + 1) * c.TS + 1]),
                    inc=sem, n=16)

        def wait_x(seq, si):
            slot = gslot(seq, si)
            b.wt("T", f"sx{slot}")
            b.wt("V", f"sx{slot}")
            b.wt("T", "V")

        def store_y(seq, si):
            slot = gslot(seq, si)
            b.wt("Y", "V")
            b.em("Y", lambda e, seq=seq, si=si, slot=slot: e.dma_start(
                out=y_out[seq, :, :, si * c.TS:(si + 1) * c.TS],
                in_=ring[slot][:, :, 1:TSH - 1]), inc=f"so{slot}", n=16)
            b.release(f"slotfree{slot}", f"so{slot}")

        def ffn2_of(seq, si):
            slot = gslot(seq, si)
            if si > 0:
                b.em("V", lambda e, slot=slot, sv=(si - 1) % 2: e.tensor_copy(
                    out=ring[slot][:, :, 0:1], in_=sliver[sv][:]), inc="V")
            if si + 1 < c.NSUP:
                nslot = gslot(seq, si + 1)
                b.em("V", lambda e, slot=slot, nslot=nslot: e.tensor_copy(
                    out=ring[slot][:, :, TSH - 1:TSH], in_=ring[nslot][:, :, 1:2]),
                    inc="V")
            ffn(seq, si, slot, 1, si == 0, si == c.NSUP - 1)
            store_y(seq, si)

        if stage == 0:
            for seq in range(c.NSEQ):
                b.em("Y", lambda e, seq=seq: e.dma_start(
                    out=y_out[seq], in_=x_in[seq]), inc="so0", n=16)
            b.wt("Y", "so0")
        for seq in range(c.NSEQ if stage > 0 else 0):
            if seq > 0:
                b.wt("V", "T")
                b.em("V", lambda e: e.memset(kv[:], 0.0), inc="V")
                b.em("V", lambda e: e.memset(ks[:], 0.0), inc="V")
                b.em("V", lambda e: e.memset(ksb[:], 0.0), inc="V")
            load_x(seq, 0)
            if c.NSUP > 1:
                load_x(seq, 1)
            for si in range(c.NSUP):
                wait_x(seq, si)
                if stage != 9:
                    ffn(seq, si, gslot(seq, si), 0, si == 0, si == c.NSUP - 1)
                if stage in (1, 9):
                    store_y(seq, si)
                    if si + 2 < c.NSUP:
                        load_x(seq, si + 2)
                    continue
                attn(seq, si, gslot(seq, si))
                if stage == 2:
                    store_y(seq, si)
                    if si + 2 < c.NSUP:
                        load_x(seq, si + 2)
                    continue
                if si > 0:
                    ffn2_of(seq, si - 1)
                slot = gslot(seq, si)
                b.em("V", lambda e, slot=slot, sv=si % 2: e.tensor_copy(
                    out=sliver[sv][:], in_=ring[slot][:, :, TSH - 2:TSH - 1]),
                    inc="V")
                if si + 2 < c.NSUP:
                    load_x(seq, si + 2)
            if stage == 3:
                ffn2_of(seq, c.NSUP - 1)

        for i in range(3):
            b.wt("Y", f"so{i}")

        with nc.Block() as block:
            @block.tensor
            def _(eng):
                for f in b.Q["T"]:
                    f(eng)

            @block.vector
            def _(eng):
                for f in b.Q["V"]:
                    f(eng)

            @block.scalar
            def _(eng):
                for f in b.Q["S"]:
                    f(eng)

            @block.gpsimd
            def _(eng):
                for f in b.Q["G"]:
                    f(eng)

            @block.sync
            def _(eng):
                for f in b.Q["Y"]:
                    f(eng)

    return nc


def _prep_consts():
    idf = np.eye(128, dtype=np.float32)
    idb = np.eye(128, dtype=ml_dtypes.bfloat16)
    mask = np.triu(np.ones((128, 128), dtype=np.float32))
    onesb = np.ones((128, 1), dtype=ml_dtypes.bfloat16)
    return idf, idb, mask, onesb


def _prep_weights(Wq, bq, Wk, bk, Wv, bv, Wo, bo, w1, cw, cb, w2):
    d = {}

    def wt_t(W):
        return np.ascontiguousarray(W.T.reshape(8, 128, -1).transpose(1, 0, 2))

    d["wq_in"], d["wk_in"], d["wv_in"], d["wo_in"] = map(wt_t, (Wq, Wk, Wv, Wo))
    d["w1_in"] = np.ascontiguousarray(w1.T.reshape(8, 128, 4096).transpose(1, 0, 2))
    d["w2_in"] = np.ascontiguousarray(w2.T.reshape(32, 128, 1024).transpose(1, 0, 2))

    def bvec(v):
        return np.ascontiguousarray(v.reshape(8, 128).T)

    d["bq_in"], d["bk_in"], d["bo_in"] = bvec(bq), bvec(bk), bvec(bo)
    d["bv_in"] = np.ascontiguousarray(bv.reshape(1, 1024))
    d["cw_in"] = np.ascontiguousarray(
        cw[:, 0, :].T.reshape(3, 32, 128).transpose(0, 2, 1))
    d["cb_in"] = np.ascontiguousarray(cb.reshape(32, 128).T)
    return d


_CACHED = {}


def kernel(L0_attn_chunk, L0_attn_Wq, L0_attn_bq, L0_attn_Wk, L0_attn_bk,
           L0_attn_Wv, L0_attn_bv, L0_attn_Wo, L0_attn_bo,
           L0_ffn_w1, L0_ffn_conv_w, L0_ffn_conv_b, L0_ffn_w2):
    cfg = Cfg(nseq=2, tseq=4096)
    n_cores = 8

    x = np.asarray(L0_attn_chunk, dtype=np.float32)
    B, T, D = x.shape
    x_fm = np.ascontiguousarray(x.reshape(B, T, 8, 128).transpose(0, 3, 2, 1))

    wd = _prep_weights(
        np.asarray(L0_attn_Wq, np.float32), np.asarray(L0_attn_bq, np.float32),
        np.asarray(L0_attn_Wk, np.float32), np.asarray(L0_attn_bk, np.float32),
        np.asarray(L0_attn_Wv, np.float32), np.asarray(L0_attn_bv, np.float32),
        np.asarray(L0_attn_Wo, np.float32), np.asarray(L0_attn_bo, np.float32),
        np.asarray(L0_ffn_w1, np.float32), np.asarray(L0_ffn_conv_w, np.float32),
        np.asarray(L0_ffn_conv_b, np.float32), np.asarray(L0_ffn_w2, np.float32))
    idf, idb, mask, onesb = _prep_consts()
    wd.update(dict(idf_in=idf, idb_in=idb, mask_in=mask, onesb_in=onesb))

    if "nc" not in _CACHED:
        _CACHED["nc"] = build_program(cfg)
    nc = _CACHED["nc"]

    in_maps = []
    for ci in range(n_cores):
        m = dict(wd)
        m["x_in"] = np.ascontiguousarray(x_fm[ci * 2:(ci + 1) * 2])
        in_maps.append(m)

    res = run_bass_kernel_spmd(nc, in_maps, core_ids=list(range(n_cores)))
    outs = [r["y_out"] for r in res.results]
    y = np.concatenate(outs, axis=0)
    y = np.ascontiguousarray(
        y.transpose(0, 3, 2, 1).reshape(B, T, D).astype(np.float32))
    return y


# version 02
# speedup vs baseline: 1.0034x; 1.0034x over previous
import contextlib
import numpy as np
import ml_dtypes

import concourse.bass as bass
import concourse.mybir as mybir
from concourse.bass_utils import run_bass_kernel_spmd

import concourse.bass_utils as _bu
_orig_run_command = _bu.run_command
def _fast_run_command(cmd, **kw):
    import os as _os
    if not _os.environ.get("KM_BIRSIM"):
        cmd = [c.replace("--enable-birsim=true", "--enable-birsim=false")
               if isinstance(c, str) else c for c in cmd]
    return _orig_run_command(cmd, **kw)
_bu.run_command = _fast_run_command

F32 = mybir.dt.float32
BF16 = mybir.dt.bfloat16
I8 = mybir.dt.int8
AF = mybir.ActivationFunctionType
ALU = mybir.AluOpType
AX = mybir.AxisListType

ENGS = ("T", "V", "S", "G", "Y")


class Cfg:
    def __init__(self, nseq=2, tseq=4096):
        self.NSEQ = nseq
        self.TSEQ = tseq
        self.D = 1024
        self.H = 4096
        self.DC = self.D // 128
        self.HC = self.H // 128
        self.CH = 128
        self.TS = 256
        self.CPS = self.TS // self.CH
        self.NSUP = self.TSEQ // self.TS
        self.EPS_LN = 1e-5
        self.EPS_AT = 1e-6


class Builder:
    def __init__(self, nc):
        self.nc = nc
        self.Q = {e: [] for e in ENGS}
        self.sems = {}
        self.cnt = {}
        self.stack = contextlib.ExitStack()
        self.res_free = {}

    def sem(self, name):
        self.sems[name] = self.stack.enter_context(self.nc.semaphore(name))
        self.cnt[name] = 0

    def sb(self, name, shape, dtype=F32):
        return self.stack.enter_context(self.nc.sbuf_tensor(name, shape, dtype))

    def ps(self, name, shape, dtype=F32):
        return self.stack.enter_context(self.nc.psum_tensor(name, shape, dtype))

    def em(self, eng, fn, inc=None, n=1):
        if inc is not None:
            h = self.sems[inc]
            self.cnt[inc] += n
            self.Q[eng].append(lambda e, fn=fn, h=h, n=n: fn(e).then_inc(h, n))
        else:
            self.Q[eng].append(lambda e, fn=fn: fn(e))

    def wt(self, eng, sem, val=None):
        v = self.cnt[sem] if val is None else val
        h = self.sems[sem]
        self.Q[eng].append(lambda e, h=h, v=v: e.wait_ge(h, v))

    def snap(self, sem):
        return self.cnt[sem]

    def flush(self, eng):
        self.wt(eng, eng)

    def claim(self, eng, name):
        ent = self.res_free.get(name)
        if ent is not None:
            for sem, v in ent:
                self.wt(eng, sem, v)

    def release(self, name, *sems):
        self.res_free[name] = [(s, self.cnt[s]) for s in sems]


def build_program(cfg: Cfg, stage: int = 3):
    nc = bass.Bass("TRN2", target_bir_lowering=False, debug=False,
                   detect_race_conditions=False)
    c = cfg
    NSUP_T = c.NSEQ * c.NSUP
    TSH = c.TS + 2

    x_in = nc.dram_tensor("x_in", [c.NSEQ, 128, c.DC, c.TSEQ], F32, kind="ExternalInput").ap()
    y_out = nc.dram_tensor("y_out", [c.NSEQ, 128, c.DC, c.TSEQ], F32, kind="ExternalOutput").ap()
    wq_in = nc.dram_tensor("wq_in", [128, c.DC, c.D], F32, kind="ExternalInput").ap()
    wk_in = nc.dram_tensor("wk_in", [128, c.DC, c.D], F32, kind="ExternalInput").ap()
    wv_in = nc.dram_tensor("wv_in", [128, c.DC, c.D], F32, kind="ExternalInput").ap()
    wo_in = nc.dram_tensor("wo_in", [128, c.DC, c.D], F32, kind="ExternalInput").ap()
    w1_in = nc.dram_tensor("w1_in", [128, c.DC, c.H], F32, kind="ExternalInput").ap()
    w2_in = nc.dram_tensor("w2_in", [128, c.HC, c.D], F32, kind="ExternalInput").ap()
    bq_in = nc.dram_tensor("bq_in", [128, c.DC], F32, kind="ExternalInput").ap()
    bk_in = nc.dram_tensor("bk_in", [128, c.DC], F32, kind="ExternalInput").ap()
    bv_in = nc.dram_tensor("bv_in", [1, c.D], F32, kind="ExternalInput").ap()
    bo_in = nc.dram_tensor("bo_in", [128, c.DC], F32, kind="ExternalInput").ap()
    cw_in = nc.dram_tensor("cw_in", [3, 128, c.HC], F32, kind="ExternalInput").ap()
    cb_in = nc.dram_tensor("cb_in", [128, c.HC], F32, kind="ExternalInput").ap()
    idf_in = nc.dram_tensor("idf_in", [128, 128], F32, kind="ExternalInput").ap()
    idb_in = nc.dram_tensor("idb_in", [128, 128], BF16, kind="ExternalInput").ap()
    mask_in = nc.dram_tensor("mask_in", [128, 128], F32, kind="ExternalInput").ap()
    onesb_in = nc.dram_tensor("onesb_in", [128, 1], BF16, kind="ExternalInput").ap()

    wq_s = nc.dram_tensor("wq_s", [128, c.DC, c.D], BF16, kind="Internal").ap()
    wk_s = nc.dram_tensor("wk_s", [128, c.DC, c.D], BF16, kind="Internal").ap()
    wv_s = nc.dram_tensor("wv_s", [128, c.DC, c.D], BF16, kind="Internal").ap()
    wo_s = nc.dram_tensor("wo_s", [128, c.DC, c.D], BF16, kind="Internal").ap()
    w1q_s = nc.dram_tensor("w1q_s", [c.HC, 128, c.D], BF16, kind="Internal").ap()
    w2q_s = nc.dram_tensor("w2q_s", [c.DC, 128, c.H], BF16, kind="Internal").ap()
    NROWK = 8
    rows_s = nc.dram_tensor("rows_s", [NROWK, 2, NSUP_T, c.TS + 4], F32, kind="Internal").ap()
    RK_FA, RK_FB, RK_FRS, RK_FS2, RK_FSS2, RK_AA, RK_AB, RK_RD = range(8)

    b = Builder(nc)
    with b.stack:
        for s in ["T", "V", "S", "G"]:
            b.sem(s)
        for i in range(3):
            b.sem(f"sx{i}")
            b.sem(f"so{i}")
            b.sem(f"sw1{i}")
        for i in range(2):
            b.sem(f"sw2{i}")
        b.sem("swa")
        b.sem("srow")
        b.sem("sbc")
        b.sem("scu")
        b.sem("sch")
        b.sem("sck")
        b.sem("spro")

        ring = [b.sb(f"ring{i}", [128, c.DC, TSH], F32) for i in range(3)]
        hr = b.sb("hr", [128, c.HC, TSH], F32)
        hrf = hr[:].rearrange("p a t -> p (a t)")
        uq8s = [b.sb(f"uq8_{i}", [128, c.DC, TSH], I8) for i in range(2)]
        uqbs = [b.sb(f"uqb_{i}", [128, c.DC, TSH], BF16) for i in range(2)]
        tmpUd = b.sb("tmpUd", [128, c.DC, TSH], F32)
        tmpUf = tmpUd[:].rearrange("p a t -> p (a t)")
        hq8 = b.sb("hq8", [128, c.HC, c.TS], I8)
        hq8f = hq8[:].rearrange("p a t -> p (a t)")
        kvq = b.sb("kvq", [128, 8192], BF16)
        kvb3 = kvq[:].rearrange("p (a t) -> p a t", a=c.DC)
        hqb3 = kvq[:].rearrange("p (a t) -> p a t", a=c.HC)
        w1sl = [b.sb(f"w1sl{i}", [128, c.DC, 128], BF16) for i in range(3)]
        w2sl = [b.sb(f"w2sl{i}", [128, c.HC, 128], BF16) for i in range(2)]
        wasl = b.sb("wasl", [128, c.DC, 512], BF16)
        kv = b.sb("kv", [128, c.DC, c.D], F32)
        ks = b.sb("ks", [128, c.DC], F32)
        ksb = b.sb("ksb", [128, c.DC], BF16)
        qf = b.sb("qf", [128, c.DC, c.TS], BF16)
        kf = b.sb("kf", [128, c.DC, c.TS], BF16)
        ktm = b.sb("ktm", [128, c.CPS, c.D], BF16)
        vtm = b.sb("vtm", [128, c.CPS, c.D], BF16)
        uA = b.sb("uA", [128, c.DC, c.TS], BF16)
        afm = uA
        AT = b.sb("ATt", [128, 3, 128], BF16)
        bcFAs = [b.sb(f"bcFA{i}", [128, TSH], F32) for i in range(2)]
        bcFBs = [b.sb(f"bcFB{i}", [128, TSH], F32) for i in range(2)]
        bcFRSs = [b.sb(f"bcFRS{i}", [128, TSH], F32) for i in range(2)]
        bcFS2 = b.sb("bcFS2", [128, c.TS], F32)
        bcFSS2 = b.sb("bcFSS2", [128, c.TS], F32)
        bcAA = b.sb("bcAA", [128, c.TS], F32)
        bcAB = b.sb("bcAB", [128, c.TS], F32)
        bcRD = b.sb("bcRD", [128, 128], F32)
        bvbc = b.sb("bvbc", [128, c.D], F32)
        tmpa = b.sb("tmpa", [128, TSH], F32)
        tmpb = b.sb("tmpb", [128, TSH], F32)
        tmpc = b.sb("tmpc", [128, TSH], F32)
        tmpd = b.sb("tmpd", [128, TSH], F32)
        stats_t = [b.sb(f"stat{i}", [128, 18], F32) for i in range(2)]
        rowsts = [b.sb(f"rowst{i}", [128, 21], F32) for i in range(2)]
        kswork = b.sb("kswork", [128, c.DC], F32)
        sliver = [b.sb(f"sliver{i}", [128, c.DC, 1], F32) for i in range(2)]
        convL = b.sb("convL", [128, c.HC], F32)
        convM = b.sb("convM", [128, c.HC], F32)
        convR = b.sb("convR", [128, c.HC], F32)
        convB = b.sb("convB", [128, c.HC], F32)
        batt = {}
        for nm in ["bq", "bk", "bo"]:
            batt[nm] = b.sb(nm, [128, c.DC], F32)
            batt["n" + nm] = b.sb("n" + nm, [128, c.DC], F32)
        idf = b.sb("idf", [128, 128], F32)
        idb = b.sb("idb", [128, 128], BF16)
        maskt = b.sb("maskt", [128, 128], F32)
        onesb = b.sb("onesb", [128, 1], BF16)
        epst = b.sb("epst", [128, 1], F32)
        scal = b.sb("scal", [1, 8], F32)
        scalb = b.sb("scalb", [128, 8], F32)
        den_r = b.sb("den_r", [1, 128], F32)

        P0 = b.ps("P0", [128, 1024], F32)
        P1 = b.ps("P1", [128, 1024], F32)
        P2 = b.ps("P2", [128, TSH], F32)
        P3 = b.ps("P3", [128, TSH], F32)
        P4 = b.ps("P4", [128, TSH], F32)
        PB = b.ps("PBf", [128, 1024], BF16)
        Pm = [P2, P3, P4]
        PmN = ["P2", "P3", "P4"]

        for dst, src in [(idf, idf_in), (idb, idb_in), (maskt, mask_in),
                         (onesb, onesb_in), (convB, cb_in),
                         (batt["bq"], bq_in), (batt["bk"], bk_in), (batt["bo"], bo_in)]:
            b.em("Y", lambda e, d=dst, s=src: e.dma_start(out=d[:], in_=s),
                 inc="spro", n=16)
        for i, t in enumerate([convL, convM, convR]):
            b.em("Y", lambda e, t=t, i=i: e.dma_start(out=t[:], in_=cw_in[i]),
                 inc="spro", n=16)
        b.em("Y", lambda e: e.dma_start(out=bvbc[:], in_=bv_in.partition_broadcast(128)),
             inc="spro", n=16)
        b.wt("V", "spro")
        b.em("V", lambda e: e.memset(epst[:], c.EPS_LN))
        for i in range(2):
            b.em("V", lambda e, i=i: e.memset(rowsts[i][:], 0.0))
            b.em("V", lambda e, i=i: e.memset(stats_t[i][:], 0.0))
        b.em("V", lambda e: e.memset(kswork[:], 0.0))
        b.em("V", lambda e: e.memset(den_r[:], 0.0))
        b.em("V", lambda e: e.memset(tmpUd[:], 0.0))
        b.em("V", lambda e: e.memset(kv[:], 0.0))
        b.em("V", lambda e: e.memset(ks[:], 0.0))
        b.em("V", lambda e: e.memset(ksb[:], 0.0))
        for nm in ["bq", "bk", "bo"]:
            b.em("V", lambda e, s=batt[nm], d=batt["n" + nm]:
                 e.tensor_scalar_mul(out=d[:], in0=s[:], scalar1=-1.0), inc="V")

        QW = 2048
        for wsrc, wdst in [(wq_in, wq_s), (wk_in, wk_s), (wv_in, wv_s), (wo_in, wo_s)]:
            for qtr in range(4):
                sl = slice(qtr * 256, (qtr + 1) * 256)
                b.wt("Y", "V")
                b.em("Y", lambda e, wsrc=wsrc, sl=sl: e.dma_start(
                    out=hrf[:, 0:QW].rearrange("p (a t) -> p a t", a=c.DC),
                    in_=wsrc[:, :, sl]), inc="spro", n=16)
                b.wt("V", "spro")
                b.em("V", lambda e: e.tensor_copy(
                    out=kvq[:, 0:QW], in_=hrf[:, 0:QW]), inc="V")
                b.wt("Y", "V")
                b.em("Y", lambda e, wdst=wdst, sl=sl: e.dma_start(
                    out=wdst[:, :, sl],
                    in_=kvq[:, 0:QW].rearrange("p (a t) -> p a t", a=c.DC)),
                    inc="spro", n=16)
        b.wt("V", "spro")

        b.em("V", lambda e: e.memset(tmpa[:, 0:2], 0.0), inc="V")

        def absacc(wsrc, n_m, m_w, col):
            flat_w = m_w * 128
            for m in range(n_m):
                b.wt("Y", "V")
                b.em("Y", lambda e, m=m, wsrc=wsrc, flat_w=flat_w, m_w=m_w:
                     e.dma_start(out=hrf[:, 0:flat_w].rearrange(
                         "p (a t) -> p a t", a=m_w),
                         in_=wsrc[:, :, m * 128:(m + 1) * 128]), inc="spro", n=16)
                b.wt("V", "spro")
                b.em("V", lambda e, flat_w=flat_w: e.tensor_reduce(
                    out=tmpb[:, 0:1], in_=hrf[:, 0:flat_w],
                    axis=AX.X, op=ALU.add, apply_absolute_value=True), inc="V")
                b.flush("V")
                b.em("V", lambda e, col=col: e.tensor_add(
                    out=tmpa[:, col:col + 1], in0=tmpa[:, col:col + 1],
                    in1=tmpb[:, 0:1]), inc="V")
                b.flush("V")

        absacc(w1_in, c.HC, c.DC, 0)
        absacc(w2_in, c.DC, c.HC, 1)
        b.wt("T", "V")
        b.claim("T", "P2")
        b.em("T", lambda e: e.transpose(out=P2[0:1, 0:128], in_=tmpa[:, 0:1],
                                        identity=idf[:]), inc="T")
        b.em("T", lambda e: e.transpose(out=P2[0:1, 128:256], in_=tmpa[:, 1:2],
                                        identity=idf[:]), inc="T")
        b.wt("V", "T")
        b.em("V", lambda e: e.tensor_reduce(out=scal[0:1, 2:3], in_=P2[0:1, 0:128],
                                            axis=AX.X, op=ALU.add), inc="V")
        b.em("V", lambda e: e.tensor_reduce(out=scal[0:1, 3:4], in_=P2[0:1, 128:256],
                                            axis=AX.X, op=ALU.add), inc="V")
        b.release("P2", "V")
        b.flush("V")

        def final_scale(col_sum, col_s, n_elems, out_col_m):
            b.em("V", lambda e: e.tensor_scalar(
                out=scal[0:1, col_s:col_s + 1], in0=scal[0:1, col_sum:col_sum + 1],
                scalar1=1.0 / n_elems, scalar2=1e-5, op0=ALU.mult, op1=ALU.max),
                inc="V")
            b.flush("V")
            b.em("V", lambda e: e.tensor_copy(
                out=scal[0:1, out_col_m:out_col_m + 1],
                in_=scal[0:1, col_s:col_s + 1]), inc="V")
            b.flush("V")
            b.em("V", lambda e: e.reciprocal(
                out=scal[0:1, col_s:col_s + 1], in_=scal[0:1, col_s:col_s + 1]),
                inc="V")
            b.flush("V")

        final_scale(2, 0, c.D * c.H, 4)
        final_scale(3, 1, c.D * c.H, 5)
        b.em("V", lambda e: e.tensor_scalar_mul(out=scal[0:1, 6:7], in0=scal[0:1, 4:5],
                                                scalar1=1.0 / 127.0), inc="V")
        b.em("V", lambda e: e.tensor_scalar_mul(out=scal[0:1, 7:8], in0=scal[0:1, 5:6],
                                                scalar1=0.5 / 127.0), inc="V")
        b.flush("V")
        b.wt("G", "V")
        b.em("G", lambda e: e.dma_start(out=rows_s[0, 1, 0, 0:8], in_=scal[0:1, 0:8]),
             inc="srow", n=16)
        b.wt("G", "srow")
        b.em("G", lambda e: e.dma_start(
            out=scalb[:], in_=rows_s[0, 1, 0, 0:8].partition_broadcast(128)),
            inc="sbc", n=16)
        b.wt("V", "sbc")

        def ternarize(wsrc, wdst, n_m, m_w, s_col):
            flat_w = m_w * 128
            for m in range(n_m):
                b.wt("Y", "G")
                b.wt("Y", "V")
                b.wt("Y", "spro")
                b.em("Y", lambda e, m=m, wsrc=wsrc, flat_w=flat_w, m_w=m_w:
                     e.dma_start(out=hrf[:, 0:flat_w].rearrange(
                         "p (a t) -> p a t", a=m_w),
                         in_=wsrc[:, :, m * 128:(m + 1) * 128]), inc="spro", n=16)
                b.wt("V", "spro")
                b.em("V", lambda e, flat_w=flat_w, s_col=s_col: e.tensor_scalar(
                    out=hrf[:, 0:flat_w], in0=hrf[:, 0:flat_w],
                    scalar1=scalb[:, s_col:s_col + 1], scalar2=None,
                    op0=ALU.mult), inc="V")
                b.flush("V")
                b.em("V", lambda e, flat_w=flat_w: e.tensor_scalar(
                    out=hrf[:, 0:flat_w], in0=hrf[:, 0:flat_w],
                    scalar1=-1.0, scalar2=1.0, op0=ALU.max, op1=ALU.min), inc="V")
                b.flush("V")
                b.em("V", lambda e, flat_w=flat_w: e.tensor_copy(
                    out=hq8f[:, 0:flat_w], in_=hrf[:, 0:flat_w]), inc="V")
                b.wt("G", "V")
                b.em("G", lambda e, flat_w=flat_w: e.dma_start(
                    out=kvq[:, 0:flat_w], in_=hq8f[:, 0:flat_w]), inc="G", n=16)
                b.wt("Y", "G")
                b.em("Y", lambda e, m=m, wdst=wdst, flat_w=flat_w: e.dma_start(
                    out=wdst[m], in_=kvq[:, 0:flat_w]), inc="spro", n=16)

        ternarize(w1_in, w1q_s, c.HC, c.DC, 0)
        ternarize(w2_in, w2q_s, c.DC, c.HC, 1)
        b.wt("V", "spro")
        b.wt("T", "spro")
        b.wt("G", "spro")

        def stats_rows(seq, sup_i, slot, cols_lo, cols_hi, kinds, is_ffn, sub_idx):
            n_tok = cols_hi - cols_lo
            nt = (n_tok + 127) // 128
            src = ring[slot]
            row_i = seq * c.NSUP + sup_i
            stat = stats_t[sub_idx]
            rowst = rowsts[sub_idx]
            st_ = lambda f: stat[:, f * 3:f * 3 + nt]
            rw_ = lambda j: rowst[:, j * 3:j * 3 + nt]
            b.wt("T", "V")
            b.wt("T", "S")
            for tc in range(nt):
                lo = cols_lo + tc * 128
                hi = min(cols_lo + (tc + 1) * 128, cols_hi)
                w = hi - lo
                P = P0 if tc % 2 == 0 else P1
                nmP = "P0" if tc % 2 == 0 else "P1"
                rgn = (tc % 2) * 1024
                b.claim("T", nmP)
                for fc in range(c.DC):
                    b.em("T", lambda e, P=P, fc=fc, lo=lo, hi=hi, w=w, src=src:
                         e.transpose(out=P[0:w, fc * 128:(fc + 1) * 128],
                                     in_=src[:, fc, lo:hi], identity=idf[:]),
                         inc="T" if fc == c.DC - 1 else None)
                tT = b.snap("T")
                b.wt("V", "T", tT)
                b.wt("S", "T", tT)
                b.em("V", lambda e, P=P, w=w, tc=tc, stat=stat: e.tensor_reduce(
                    out=stat[0:w, tc:tc + 1], in_=P[0:w, :], axis=AX.X,
                    op=ALU.add), inc="V")
                b.wt("S", "V")
                b.em("S", lambda e, P=P, w=w, rgn=rgn: e.activation(
                    out=tmpUf[0:w, rgn:rgn + 1024], in_=P[0:w, :],
                    func=AF.Square), inc="S")
                b.wt("V", "S")
                b.em("V", lambda e, w=w, tc=tc, rgn=rgn, stat=stat: e.tensor_reduce(
                    out=stat[0:w, 3 + tc:4 + tc], in_=tmpUf[0:w, rgn:rgn + 1024],
                    axis=AX.X, op=ALU.add), inc="V")
                if is_ffn:
                    b.em("V", lambda e, P=P, w=w, tc=tc, stat=stat: e.tensor_reduce(
                        out=stat[0:w, 6 + tc:7 + tc], in_=P[0:w, :], axis=AX.X,
                        op=ALU.max), inc="V")
                    b.em("V", lambda e, P=P, w=w, tc=tc, stat=stat: e.tensor_reduce(
                        out=stat[0:w, 9 + tc:10 + tc], in_=P[0:w, :], axis=AX.X,
                        op=ALU.min), inc="V")
                b.release(nmP, "V", "S")
            b.flush("V")
            b.claim("V", f"rowst{sub_idx}")
            b.em("V", lambda e: e.tensor_scalar_mul(
                out=st_(4), in0=st_(0), scalar1=1.0 / c.D), inc="V")
            b.flush("V")
            b.em("V", lambda e: e.tensor_mul(
                out=rw_(0), in0=st_(4), in1=st_(4)), inc="V")
            b.flush("V")
            b.em("V", lambda e: e.scalar_tensor_tensor(
                out=st_(5), in0=st_(1), scalar=1.0 / c.D,
                in1=rw_(0), op0=ALU.mult, op1=ALU.subtract), inc="V")
            b.flush("V")
            b.wt("S", "V")
            b.em("S", lambda e: e.activation(
                out=st_(5), in_=st_(5), func=AF.Sqrt,
                bias=epst[:, 0:1], scale=1.0), inc="S")
            b.wt("V", "S")
            b.em("V", lambda e: e.reciprocal(out=st_(5), in_=st_(5)), inc="V")
            b.flush("V")
            if is_ffn:
                b.em("V", lambda e: e.tensor_sub(
                    out=rw_(1), in0=st_(2), in1=st_(4)), inc="V")
                b.em("V", lambda e: e.tensor_sub(
                    out=rw_(2), in0=st_(4), in1=st_(3)), inc="V")
                b.flush("V")
                b.em("V", lambda e: e.tensor_max(
                    out=rw_(1), in0=rw_(1), in1=rw_(2)), inc="V")
                b.flush("V")
                b.em("V", lambda e: e.tensor_mul(
                    out=rw_(1), in0=rw_(1), in1=st_(5)), inc="V")
                b.flush("V")
                b.em("V", lambda e: e.tensor_scalar_max(
                    out=rw_(1), in0=rw_(1), scalar1=1e-5), inc="V")
                b.flush("V")
                b.em("V", lambda e: e.reciprocal(out=rw_(3), in_=rw_(1)), inc="V")
                b.flush("V")
                b.em("V", lambda e: e.tensor_scalar_mul(
                    out=rw_(3), in0=rw_(3), scalar1=127.0), inc="V")
                b.flush("V")
                b.em("V", lambda e: e.tensor_scalar(
                    out=rw_(4), in0=rw_(1),
                    scalar1=scalb[:, 6:7], scalar2=None, op0=ALU.mult), inc="V")
                b.em("V", lambda e: e.tensor_mul(
                    out=rw_(5), in0=st_(5), in1=rw_(3)), inc="V")
                b.flush("V")
                b.em("V", lambda e: e.scalar_tensor_tensor(
                    out=rw_(6), in0=st_(4), scalar=-1.0,
                    in1=rw_(5), op0=ALU.mult, op1=ALU.mult), inc="V")
                b.flush("V")
                emitted = [(RK_FA, 5), (RK_FB, 6), (RK_FRS, 4)]
            else:
                b.em("V", lambda e: e.tensor_copy(out=rw_(5), in_=st_(5)), inc="V")
                b.flush("V")
                b.em("V", lambda e: e.scalar_tensor_tensor(
                    out=rw_(6), in0=st_(4), scalar=-1.0,
                    in1=rw_(5), op0=ALU.mult, op1=ALU.mult), inc="V")
                b.flush("V")
                emitted = [(RK_AA, 5), (RK_AB, 6)]
            b.wt("G", "V")
            for tc in range(nt):
                w = min(cols_lo + (tc + 1) * 128, cols_hi) - (cols_lo + tc * 128)
                for rk, colj in emitted:
                    b.em("G", lambda e, rk=rk, colj=colj, w=w, tc=tc, row_i=row_i,
                         sub_idx=sub_idx, rowst=rowst: e.dma_start(
                        out=rows_s[rk, sub_idx, row_i, tc * 128:tc * 128 + w],
                        in_=rowst[0:w, colj * 3 + tc:colj * 3 + tc + 1]),
                        inc="srow", n=16)
            b.release(f"rowst{sub_idx}", "srow")
            b.wt("G", "srow")
            for rk, tile, wdt in kinds:
                b.em("G", lambda e, rk=rk, tile=tile, wdt=wdt, row_i=row_i,
                     sub_idx=sub_idx: e.dma_start(
                    out=tile[:, 0:wdt],
                    in_=rows_s[rk, sub_idx, row_i, 0:wdt].partition_broadcast(128)),
                    inc="sbc", n=16)

        front_state = {}

        def ffn_front(seq, sup_i, slot, sub_idx, first_sup, last_sup):
            uq8 = uq8s[sub_idx]
            uqb = uqbs[sub_idx]
            b.wt("T", f"sx{slot}")
            b.wt("V", f"sx{slot}")
            stats_rows(seq, sup_i, slot, 0, TSH,
                       [(RK_FA, bcFAs[sub_idx], TSH), (RK_FB, bcFBs[sub_idx], TSH),
                        (RK_FRS, bcFRSs[sub_idx], TSH)], True, sub_idx)
            vbc = b.snap("sbc")
            b.wt("V", "sbc", vbc)
            b.claim("V", f"uq8buf{sub_idx}")
            b.em("V", lambda e, slot=slot, sub_idx=sub_idx: e.tensor_tensor(
                out=tmpUd[:], in0=ring[slot][:],
                in1=bcFAs[sub_idx][:, 0:TSH].unsqueeze(1)
                    .broadcast_to([128, c.DC, TSH]),
                op=ALU.mult), inc="V")
            b.flush("V")
            b.em("V", lambda e, uq8=uq8, sub_idx=sub_idx: e.tensor_tensor(
                out=uq8[:], in0=tmpUd[:],
                in1=bcFBs[sub_idx][:, 0:TSH].unsqueeze(1)
                    .broadcast_to([128, c.DC, TSH]),
                op=ALU.add), inc="V")
            if first_sup:
                b.em("V", lambda e, uq8=uq8: e.memset(uq8[:, :, 0:1], 0), inc="V")
            if last_sup:
                b.em("V", lambda e, uq8=uq8: e.memset(uq8[:, :, TSH - 1:TSH], 0),
                     inc="V")
            b.wt("G", "V")
            b.claim("G", f"uqbbuf{sub_idx}")
            b.em("G", lambda e, uqb=uqb, uq8=uq8: e.dma_start(out=uqb[:], in_=uq8[:]),
                 inc="scu", n=16)
            b.release(f"uq8buf{sub_idx}", "scu")
            front_state[(seq, sup_i, sub_idx)] = (vbc, b.snap("scu"))

        def ffn_back(seq, sup_i, slot, sub_idx):
            uqb = uqbs[sub_idx]
            bcFRS = bcFRSs[sub_idx]
            vbc, scu_t = front_state.pop((seq, sup_i, sub_idx))
            row_i = seq * c.NSUP + sup_i
            for m in range(c.HC):
                wsl = w1sl[m % 3]
                wsem = f"sw1{m % 3}"
                b.claim("Y", f"w1c{m % 3}")
                b.em("Y", lambda e, m=m, wsl=wsl: e.dma_start(
                    out=wsl[:].rearrange("p a t -> p (a t)"), in_=w1q_s[m]),
                    inc=wsem, n=16)
                Pn = PmN[m % 3]
                P = Pm[m % 3]
                b.claim("T", Pn)
                b.wt("T", wsem)
                if m == 0:
                    b.wt("T", "scu", scu_t)
                for k in range(c.DC):
                    b.em("T", lambda e, P=P, k=k, wsl=wsl, uqb=uqb: e.matmul(
                        out=P[:, 0:TSH], lhsT=wsl[:, k, :], rhs=uqb[:, k, :],
                        start=(k == 0), stop=(k == c.DC - 1)),
                        inc="T" if k == c.DC - 1 else None)
                b.release(f"w1c{m % 3}", "T")
                if m == c.HC - 1:
                    b.release(f"uqbbuf{sub_idx}", "T")
                b.wt("V", "T")
                if m == 0:
                    b.wt("V", "sbc", vbc)
                b.em("V", lambda e, P=P, m=m, bcFRS=bcFRS: e.tensor_mul(
                    out=hr[:, m, :], in0=P[:, 0:TSH], in1=bcFRS[:, 0:TSH]), inc="V")
                b.release(Pn, "V")
                b.wt("S", "V")
                ct = tmpb if m % 2 == 0 else tmpc
                b.em("S", lambda e, m=m, ct=ct: e.activation(
                    out=ct[:, 0:c.TS], in_=hr[:, m, 1:TSH - 1], func=AF.Identity,
                    bias=convB[:, m:m + 1], scale=convM[:, m:m + 1]), inc="S")
                b.wt("V", "S")
                b.em("V", lambda e, m=m, ct=ct: e.scalar_tensor_tensor(
                    out=ct[:, 0:c.TS], in0=hr[:, m, 0:TSH - 2],
                    scalar=convL[:, m:m + 1], in1=ct[:, 0:c.TS],
                    op0=ALU.mult, op1=ALU.add), inc="V")
                b.flush("V")
                b.em("V", lambda e, m=m, ct=ct: e.scalar_tensor_tensor(
                    out=ct[:, 0:c.TS], in0=hr[:, m, 2:TSH],
                    scalar=convR[:, m:m + 1], in1=ct[:, 0:c.TS],
                    op0=ALU.mult, op1=ALU.add), inc="V")
                b.wt("S", "V")
                b.em("S", lambda e, m=m, ct=ct: e.activation(
                    out=hr[:, m, 0:c.TS], in_=ct[:, 0:c.TS], func=AF.Silu), inc="S")
            b.wt("V", "S")
            b.em("V", lambda e: e.tensor_reduce(
                out=tmpa[:, 0:c.TS], in_=hr[:, :, 0:c.TS].transpose([0, 2, 1]),
                axis=AX.X, op=ALU.max, apply_absolute_value=True), inc="V")
            b.wt("T", "V")
            b.claim("T", "P0")
            for tc in range(c.CPS):
                b.em("T", lambda e, tc=tc: e.transpose(
                    out=P0[:, tc * 128:(tc + 1) * 128],
                    in_=tmpa[:, tc * 128:(tc + 1) * 128], identity=idf[:]),
                    inc="T" if tc == c.CPS - 1 else None)
            b.wt("V", "T")
            rowst = rowsts[sub_idx]
            b.claim("V", f"rowst{sub_idx}")
            nt = c.CPS
            rw_ = lambda j: rowst[:, j * 3:j * 3 + nt]
            for tc in range(c.CPS):
                b.em("V", lambda e, tc=tc, rowst=rowst: e.tensor_reduce(
                    out=rowst[:, tc:tc + 1], in_=P0[:, tc * 128:(tc + 1) * 128],
                    axis=AX.X, op=ALU.max), inc="V")
            b.release("P0", "V")
            b.flush("V")
            b.em("V", lambda e, rw_=rw_: e.tensor_scalar_max(
                out=rw_(0), in0=rw_(0), scalar1=1e-5), inc="V")
            b.flush("V")
            b.em("V", lambda e, rw_=rw_: e.reciprocal(out=rw_(1), in_=rw_(0)),
                 inc="V")
            b.flush("V")
            b.em("V", lambda e, rw_=rw_: e.tensor_scalar_mul(
                out=rw_(1), in0=rw_(1), scalar1=127.0), inc="V")
            b.em("V", lambda e, rw_=rw_: e.tensor_scalar(
                out=rw_(2), in0=rw_(0),
                scalar1=scalb[:, 7:8], scalar2=None, op0=ALU.mult), inc="V")
            b.flush("V")
            b.wt("G", "V")
            for tc in range(c.CPS):
                b.em("G", lambda e, tc=tc, row_i=row_i, sub_idx=sub_idx,
                     rowst=rowst: e.dma_start(
                    out=rows_s[RK_FS2, sub_idx, row_i, tc * 128:(tc + 1) * 128],
                    in_=rowst[:, 3 + tc:4 + tc]), inc="srow", n=16)
                b.em("G", lambda e, tc=tc, row_i=row_i, sub_idx=sub_idx,
                     rowst=rowst: e.dma_start(
                    out=rows_s[RK_FSS2, sub_idx, row_i, tc * 128:(tc + 1) * 128],
                    in_=rowst[:, 6 + tc:7 + tc]), inc="srow", n=16)
            b.release(f"rowst{sub_idx}", "srow")
            b.wt("G", "srow")
            b.em("G", lambda e, row_i=row_i, sub_idx=sub_idx: e.dma_start(
                out=bcFS2[:, 0:c.TS],
                in_=rows_s[RK_FS2, sub_idx, row_i, 0:c.TS].partition_broadcast(128)),
                inc="sbc", n=16)
            b.em("G", lambda e, row_i=row_i, sub_idx=sub_idx: e.dma_start(
                out=bcFSS2[:, 0:c.TS],
                in_=rows_s[RK_FSS2, sub_idx, row_i, 0:c.TS].partition_broadcast(128)),
                inc="sbc", n=16)
            b.wt("V", "sbc")
            b.claim("V", "hq8buf")
            b.em("V", lambda e: e.tensor_tensor(
                out=hq8[:], in0=hr[:, :, 0:c.TS],
                in1=bcFS2[:, 0:c.TS].unsqueeze(1).broadcast_to([128, c.HC, c.TS]),
                op=ALU.mult), inc="V")
            vhq = b.snap("V")
            b.claim("G", "kvqbuf")
            b.wt("G", "V", vhq)
            b.em("G", lambda e: e.dma_start(out=hqb3[:], in_=hq8[:]), inc="sch",
                 n=16)
            b.release("hq8buf", "sch")
            sch_t = b.snap("sch")
            for m in range(c.DC):
                wsl = w2sl[m % 2]
                wsem = f"sw2{m % 2}"
                b.claim("Y", f"w2c{m % 2}")
                b.em("Y", lambda e, m=m, wsl=wsl: e.dma_start(
                    out=wsl[:].rearrange("p a t -> p (a t)"), in_=w2q_s[m]),
                    inc=wsem, n=16)
                Pn = "P0" if m % 2 == 0 else "P1"
                P = P0 if m % 2 == 0 else P1
                b.claim("T", Pn)
                b.wt("T", wsem)
                if m == 0:
                    b.wt("T", "sch", sch_t)
                for k in range(c.HC):
                    b.em("T", lambda e, P=P, k=k, wsl=wsl: e.matmul(
                        out=P[:, 0:c.TS], lhsT=wsl[:, k, :], rhs=hqb3[:, k, :],
                        start=(k == 0), stop=(k == c.HC - 1)),
                        inc="T" if k == c.HC - 1 else None)
                b.release(f"w2c{m % 2}", "T")
                if m == c.DC - 1:
                    b.release("kvqbuf", "T")
                b.wt("V", "T")
                ct = tmpc if m % 2 == 0 else tmpd
                b.em("V", lambda e, P=P, ct=ct: e.tensor_mul(
                    out=ct[:, 0:c.TS], in0=P[:, 0:c.TS], in1=bcFSS2[:, 0:c.TS]),
                    inc="V")
                b.release(Pn, "V")
                b.em("V", lambda e, m=m, ct=ct, slot=slot: e.tensor_add(
                    out=ring[slot][:, m, 1:TSH - 1],
                    in0=ring[slot][:, m, 1:TSH - 1], in1=ct[:, 0:c.TS]), inc="V")

        def attn(seq, sup_i, slot):
            row_i = seq * c.NSUP + sup_i
            stats_rows(seq, sup_i, slot, 1, TSH - 1,
                       [(RK_AA, bcAA, c.TS), (RK_AB, bcAB, c.TS)], False, 0)
            b.wt("V", "sbc")
            b.em("V", lambda e, slot=slot: e.tensor_tensor(
                out=tmpUd[:, :, 0:c.TS], in0=ring[slot][:, :, 1:TSH - 1],
                in1=bcAA[:, 0:c.TS].unsqueeze(1).broadcast_to([128, c.DC, c.TS]),
                op=ALU.mult), inc="V")
            b.flush("V")
            b.em("V", lambda e: e.tensor_tensor(
                out=uA[:], in0=tmpUd[:, :, 0:c.TS],
                in1=bcAB[:, 0:c.TS].unsqueeze(1).broadcast_to([128, c.DC, c.TS]),
                op=ALU.add), inc="V")
            vuA = b.snap("V")
            for (wsrc, dst, bt, nbt) in [(wq_s, qf, batt["bq"], batt["nbq"]),
                                         (wk_s, kf, batt["bk"], batt["nbk"])]:
                for hf in range(2):
                    b.wt("Y", "T")
                    b.em("Y", lambda e, wsrc=wsrc, hf=hf: e.dma_start(
                        out=wasl[:], in_=wsrc[:, :, hf * 512:(hf + 1) * 512]),
                        inc="swa", n=16)
                    b.wt("T", "swa")
                    b.wt("T", "V", vuA)
                    for ml in range(4):
                        m = hf * 4 + ml
                        Pn = "P2" if m % 2 == 0 else "P3"
                        P = P2 if m % 2 == 0 else P3
                        b.claim("T", Pn)
                        for k in range(c.DC):
                            b.em("T", lambda e, P=P, k=k, ml=ml: e.matmul(
                                out=P[:, 0:c.TS],
                                lhsT=wasl[:, k, ml * 128:(ml + 1) * 128],
                                rhs=uA[:, k, :], start=(k == 0),
                                stop=(k == c.DC - 1)),
                                inc="T" if k == c.DC - 1 else None)
                        rt = tmpb if m % 2 == 0 else tmpa
                        et = tmpc if m % 2 == 0 else tmpd
                        b.wt("S", "T")
                        b.em("S", lambda e, P=P, m=m, bt=bt, rt=rt: e.activation(
                            out=rt[:, 0:c.TS], in_=P[:, 0:c.TS], func=AF.Relu,
                            bias=bt[:, m:m + 1], scale=1.0), inc="S")
                        b.em("S", lambda e, P=P, m=m, nbt=nbt, et=et: e.activation(
                            out=et[:, 0:c.TS], in_=P[:, 0:c.TS], func=AF.Relu,
                            bias=nbt[:, m:m + 1], scale=-1.0), inc="S")
                        b.release(Pn, "S")
                        b.wt("V", "S")
                        b.em("V", lambda e, et=et: e.tensor_scalar_mul(
                            out=et[:, 0:c.TS], in0=et[:, 0:c.TS], scalar1=-1.0),
                            inc="V")
                        b.wt("S", "V")
                        b.em("S", lambda e, et=et: e.activation(
                            out=et[:, 0:c.TS], in_=et[:, 0:c.TS], func=AF.Exp,
                            scale=1.0), inc="S")
                        b.wt("V", "S")
                        b.em("V", lambda e, dst=dst, m=m, rt=rt, et=et: e.tensor_add(
                            out=dst[:, m, :], in0=rt[:, 0:c.TS],
                            in1=et[:, 0:c.TS]), inc="V")
            for nh in range(2):
                b.wt("Y", "T")
                b.em("Y", lambda e, nh=nh: e.dma_start(
                    out=wasl[:], in_=wv_s[:, :, nh * 512:(nh + 1) * 512]),
                    inc="swa", n=16)
                b.wt("T", "swa")
                b.wt("T", "V", vuA)
                for tc in range(c.CPS):
                    b.claim("T", "P0")
                    for k in range(c.DC):
                        b.em("T", lambda e, tc=tc, k=k: e.matmul(
                            out=P0[:, 0:512],
                            lhsT=uA[:, k, tc * 128:(tc + 1) * 128],
                            rhs=wasl[:, k, :], start=(k == 0),
                            stop=(k == c.DC - 1)),
                            inc="T" if k == c.DC - 1 else None)
                    b.wt("V", "T")
                    b.em("V", lambda e, tc=tc, nh=nh: e.tensor_add(
                        out=vtm[:, tc, nh * 512:(nh + 1) * 512], in0=P0[:, 0:512],
                        in1=bvbc[:, nh * 512:(nh + 1) * 512]), inc="V")
                    b.release("P0", "V")
            b.wt("T", "V")
            b.claim("T", "PBf")
            for tc in range(c.CPS):
                for fc in range(c.DC):
                    b.em("T", lambda e, tc=tc, fc=fc: e.transpose(
                        out=PB[:, fc * 128:(fc + 1) * 128],
                        in_=kf[:, fc, tc * 128:(tc + 1) * 128], identity=idb[:]),
                        inc="T" if fc == c.DC - 1 else None)
                b.wt("S", "T")
                b.em("S", lambda e, tc=tc: e.copy(out=ktm[:, tc, :], in_=PB[:, 0:c.D]),
                     inc="S")
                if tc < c.CPS - 1:
                    b.wt("T", "S")
            b.release("PBf", "S")
            b.claim("G", "kvqbuf")
            b.em("G", lambda e: e.dma_start(out=kvb3[:], in_=kv[:]), inc="sck", n=16)
            sck_t = b.snap("sck")
            ai_base = 0
            for ch in range(c.CPS):
                ccols = slice(ch * 128, (ch + 1) * 128)
                at_idx = {}
                for sb in range(ch + 1):
                    ai = ai_base + sb
                    at_idx[sb] = ai
                    b.claim("T", "P4")
                    for k in range(c.DC):
                        b.em("T", lambda e, sb=sb, k=k, ccols=ccols: e.matmul(
                            out=P4[:, 0:128], lhsT=kf[:, k, sb * 128:(sb + 1) * 128],
                            rhs=qf[:, k, ccols], start=(k == 0),
                            stop=(k == c.DC - 1)),
                            inc="T" if k == c.DC - 1 else None)
                    if sb == ch:
                        b.wt("V", "T")
                        b.em("V", lambda e, ai=ai: e.tensor_mul(
                            out=AT[:, ai, :], in0=P4[:, 0:128], in1=maskt[:]),
                            inc="V")
                        b.release("P4", "V")
                    else:
                        b.wt("S", "T")
                        b.em("S", lambda e, ai=ai: e.copy(
                            out=AT[:, ai, :], in_=P4[:, 0:128]), inc="S")
                        b.release("P4", "S")
                ai_base += ch + 1
                b.wt("T", "V")
                b.wt("T", "S")
                b.claim("T", "P2")
                nmm = (ch + 1) + c.DC
                i_mm = 0
                for sb in range(ch + 1):
                    i_mm += 1
                    b.em("T", lambda e, sb=sb, at_idx=at_idx, i_mm=i_mm, nmm=nmm:
                         e.matmul(out=P2[0:1, 0:128], lhsT=onesb[:],
                                  rhs=AT[:, at_idx[sb], :],
                                  start=(i_mm == 1), stop=(i_mm == nmm)))
                for k in range(c.DC):
                    i_mm += 1
                    b.em("T", lambda e, k=k, ccols=ccols, i_mm=i_mm, nmm=nmm:
                         e.matmul(out=P2[0:1, 0:128], lhsT=ksb[:, k:k + 1],
                                  rhs=qf[:, k, ccols],
                                  start=(i_mm == 1), stop=(i_mm == nmm)),
                         inc="T" if i_mm == nmm else None)
                b.wt("V", "T")
                b.em("V", lambda e: e.tensor_scalar_add(
                    out=den_r[:, :], in0=P2[0:1, 0:128], scalar1=c.EPS_AT), inc="V")
                b.flush("V")
                b.em("V", lambda e: e.reciprocal(out=den_r[:, :], in_=den_r[:, :]),
                     inc="V")
                b.flush("V")
                b.release("P2", "V")
                b.wt("G", "V")
                b.em("G", lambda e, row_i=row_i, ch=ch: e.dma_start(
                    out=rows_s[RK_RD, 0, row_i, ch * 128:(ch + 1) * 128],
                    in_=den_r[0:1, :]), inc="srow", n=16)
                b.wt("G", "srow")
                b.em("G", lambda e, row_i=row_i, ch=ch: e.dma_start(
                    out=bcRD[:, :],
                    in_=rows_s[RK_RD, 0, row_i, ch * 128:(ch + 1) * 128]
                        .partition_broadcast(128)), inc="sbc", n=16)
                b.claim("T", "P1")
                b.wt("T", "sck", sck_t)
                for m in range(c.DC):
                    n_steps = (ch + 1) + c.DC
                    step = 0
                    for sb in range(ch + 1):
                        step += 1
                        b.em("T", lambda e, sb=sb, m=m, at_idx=at_idx, step=step,
                             n_steps=n_steps: e.matmul(
                            out=P1[:, m * 128:(m + 1) * 128],
                            lhsT=vtm[:, sb, m * 128:(m + 1) * 128],
                            rhs=AT[:, at_idx[sb], :],
                            start=(step == 1), stop=(step == n_steps)))
                    for d in range(c.DC):
                        step += 1
                        b.em("T", lambda e, d=d, m=m, ccols=ccols, step=step,
                             n_steps=n_steps: e.matmul(
                            out=P1[:, m * 128:(m + 1) * 128],
                            lhsT=kvb3[:, d, m * 128:(m + 1) * 128],
                            rhs=qf[:, d, ccols],
                            start=(step == 1), stop=(step == n_steps)),
                            inc="T" if (step == n_steps and m == c.DC - 1) else None)
                b.wt("V", "T")
                b.wt("V", "sbc")
                for m in range(c.DC):
                    b.em("V", lambda e, m=m, ch=ch: e.tensor_mul(
                        out=afm[:, m, ch * 128:(ch + 1) * 128],
                        in0=P1[:, m * 128:(m + 1) * 128], in1=bcRD[:, :]), inc="V")
                b.release("P1", "V")
            b.release("kvqbuf", "T")
            for d in range(c.DC):
                Pn = "P0" if d % 2 == 0 else "P1"
                P = P0 if d % 2 == 0 else P1
                b.claim("T", Pn)
                nmm = c.CPS * 2
                i_mm = 0
                for tc in range(c.CPS):
                    for nh in range(2):
                        i_mm += 1
                        b.em("T", lambda e, P=P, d=d, tc=tc, nh=nh, i_mm=i_mm,
                             nmm=nmm: e.matmul(
                            out=P[:, nh * 512:(nh + 1) * 512],
                            lhsT=ktm[:, tc, d * 128:(d + 1) * 128],
                            rhs=vtm[:, tc, nh * 512:(nh + 1) * 512],
                            start=(tc == 0), stop=(tc == c.CPS - 1)),
                            inc="T" if i_mm == nmm else None)
                b.wt("V", "T")
                b.em("V", lambda e, P=P, d=d: e.tensor_add(
                    out=kv[:, d, :], in0=kv[:, d, :], in1=P[:, 0:c.D]), inc="V")
                b.release(Pn, "V")
            b.em("V", lambda e: e.tensor_reduce(
                out=kswork[:], in_=kf[:], axis=AX.X, op=ALU.add), inc="V")
            b.flush("V")
            b.em("V", lambda e: e.tensor_add(out=ks[:], in0=ks[:],
                                             in1=kswork[:]), inc="V")
            b.flush("V")
            b.wt("S", "V")
            b.em("S", lambda e: e.copy(out=ksb[:], in_=ks[:]), inc="S")
            b.wt("T", "V")
            for hf in range(2):
                b.wt("Y", "T")
                b.em("Y", lambda e, hf=hf: e.dma_start(
                    out=wasl[:], in_=wo_s[:, :, hf * 512:(hf + 1) * 512]),
                    inc="swa", n=16)
                b.wt("T", "swa")
                for ml in range(4):
                    m = hf * 4 + ml
                    Pn = "P2" if m % 2 == 0 else "P3"
                    P = P2 if m % 2 == 0 else P3
                    b.claim("T", Pn)
                    for k in range(c.DC):
                        b.em("T", lambda e, P=P, k=k, ml=ml: e.matmul(
                            out=P[:, 0:c.TS],
                            lhsT=wasl[:, k, ml * 128:(ml + 1) * 128],
                            rhs=afm[:, k, :], start=(k == 0),
                            stop=(k == c.DC - 1)),
                            inc="T" if k == c.DC - 1 else None)
                    b.wt("V", "T")
                    b.em("V", lambda e, P=P, m=m, slot=slot: e.scalar_tensor_tensor(
                        out=ring[slot][:, m, 1:TSH - 1], in0=P[:, 0:c.TS],
                        scalar=batt["bo"][:, m:m + 1],
                        in1=ring[slot][:, m, 1:TSH - 1], op0=ALU.add, op1=ALU.add),
                        inc="V")
                    b.release(Pn, "V")

        def gslot(seq, si):
            return (seq * c.NSUP + si) % 3

        def load_x(seq, si):
            slot = gslot(seq, si)
            sem = f"sx{slot}"
            b.claim("Y", f"slotfree{slot}")
            if c.NSUP == 1:
                b.em("Y", lambda e, seq=seq, slot=slot: e.dma_start(
                    out=ring[slot][:, :, 1:TSH - 1],
                    in_=x_in[seq, :, :, 0:c.TS]), inc=sem, n=16)
                b.wt("V", sem)
                b.em("V", lambda e, slot=slot: e.memset(ring[slot][:, :, 0:1], 0.0),
                     inc="V")
                b.em("V", lambda e, slot=slot: e.memset(
                    ring[slot][:, :, TSH - 1:TSH], 0.0), inc="V")
            elif si == 0:
                b.em("Y", lambda e, seq=seq, slot=slot: e.dma_start(
                    out=ring[slot][:, :, 1:TSH],
                    in_=x_in[seq, :, :, 0:c.TS + 1]), inc=sem, n=16)
                b.wt("V", sem)
                b.em("V", lambda e, slot=slot: e.memset(ring[slot][:, :, 0:1], 0.0),
                     inc="V")
            elif si == c.NSUP - 1:
                b.em("Y", lambda e, si=si, seq=seq, slot=slot: e.dma_start(
                    out=ring[slot][:, :, 0:TSH - 1],
                    in_=x_in[seq, :, :, si * c.TS - 1:c.TSEQ]), inc=sem, n=16)
                b.wt("V", sem)
                b.em("V", lambda e, slot=slot: e.memset(
                    ring[slot][:, :, TSH - 1:TSH], 0.0), inc="V")
            else:
                b.em("Y", lambda e, seq=seq, si=si, slot=slot: e.dma_start(
                    out=ring[slot][:],
                    in_=x_in[seq, :, :, si * c.TS - 1:(si + 1) * c.TS + 1]),
                    inc=sem, n=16)

        def wait_x(seq, si):
            slot = gslot(seq, si)
            b.wt("T", f"sx{slot}")
            b.wt("V", f"sx{slot}")
            b.wt("T", "V")

        def store_y(seq, si):
            slot = gslot(seq, si)
            b.wt("Y", "V")
            b.em("Y", lambda e, seq=seq, si=si, slot=slot: e.dma_start(
                out=y_out[seq, :, :, si * c.TS:(si + 1) * c.TS],
                in_=ring[slot][:, :, 1:TSH - 1]), inc=f"so{slot}", n=16)
            b.release(f"slotfree{slot}", f"so{slot}")

        def ffn2_front(seq, si):
            slot = gslot(seq, si)
            if si > 0:
                b.em("V", lambda e, slot=slot, sv=(si - 1) % 2: e.tensor_copy(
                    out=ring[slot][:, :, 0:1], in_=sliver[sv][:]), inc="V")
            if si + 1 < c.NSUP:
                nslot = gslot(seq, si + 1)
                b.em("V", lambda e, slot=slot, nslot=nslot: e.tensor_copy(
                    out=ring[slot][:, :, TSH - 1:TSH], in_=ring[nslot][:, :, 1:2]),
                    inc="V")
            ffn_front(seq, si, slot, 1, si == 0, si == c.NSUP - 1)

        for seq in range(c.NSEQ):
            if seq > 0:
                b.wt("V", "T")
                b.em("V", lambda e: e.memset(kv[:], 0.0), inc="V")
                b.em("V", lambda e: e.memset(ks[:], 0.0), inc="V")
                b.em("V", lambda e: e.memset(ksb[:], 0.0), inc="V")
            load_x(seq, 0)
            if c.NSUP > 1:
                load_x(seq, 1)
            for si in range(c.NSUP):
                wait_x(seq, si)
                if stage == 0:
                    store_y(seq, si)
                    if si + 2 < c.NSUP:
                        load_x(seq, si + 2)
                    continue
                if (seq, si, 0) not in front_state:
                    ffn_front(seq, si, gslot(seq, si), 0,
                              si == 0, si == c.NSUP - 1)
                ffn_back(seq, si, gslot(seq, si), 0)
                if stage >= 2:
                    attn(seq, si, gslot(seq, si))
                if stage >= 3 and si > 0:
                    ffn2_front(seq, si - 1)
                slot = gslot(seq, si)
                b.em("V", lambda e, slot=slot, sv=si % 2: e.tensor_copy(
                    out=sliver[sv][:], in_=ring[slot][:, :, TSH - 2:TSH - 1]),
                    inc="V")
                if si + 1 < c.NSUP:
                    ffn_front(seq, si + 1, gslot(seq, si + 1), 0,
                              False, si + 1 == c.NSUP - 1)
                if stage < 3:
                    store_y(seq, si)
                elif si > 0:
                    ffn_back(seq, si - 1, gslot(seq, si - 1), 1)
                    store_y(seq, si - 1)
                if si + 2 < c.NSUP:
                    load_x(seq, si + 2)
            if stage >= 3:
                ffn2_front(seq, c.NSUP - 1)
                ffn_back(seq, c.NSUP - 1, gslot(seq, c.NSUP - 1), 1)
                store_y(seq, c.NSUP - 1)

        for i in range(3):
            b.wt("Y", f"so{i}")

        with nc.Block() as block:
            @block.tensor
            def _(eng):
                for f in b.Q["T"]:
                    f(eng)

            @block.vector
            def _(eng):
                for f in b.Q["V"]:
                    f(eng)

            @block.scalar
            def _(eng):
                for f in b.Q["S"]:
                    f(eng)

            @block.gpsimd
            def _(eng):
                for f in b.Q["G"]:
                    f(eng)

            @block.sync
            def _(eng):
                for f in b.Q["Y"]:
                    f(eng)

    return nc


def _prep_consts():
    idf = np.eye(128, dtype=np.float32)
    idb = np.eye(128, dtype=ml_dtypes.bfloat16)
    mask = np.triu(np.ones((128, 128), dtype=np.float32))
    onesb = np.ones((128, 1), dtype=ml_dtypes.bfloat16)
    return idf, idb, mask, onesb


def _prep_weights(Wq, bq, Wk, bk, Wv, bv, Wo, bo, w1, cw, cb, w2):
    d = {}

    def wt_t(W):
        return np.ascontiguousarray(W.T.reshape(8, 128, -1).transpose(1, 0, 2))

    d["wq_in"], d["wk_in"], d["wv_in"], d["wo_in"] = map(wt_t, (Wq, Wk, Wv, Wo))
    d["w1_in"] = np.ascontiguousarray(w1.T.reshape(8, 128, 4096).transpose(1, 0, 2))
    d["w2_in"] = np.ascontiguousarray(w2.T.reshape(32, 128, 1024).transpose(1, 0, 2))

    def bvec(v):
        return np.ascontiguousarray(v.reshape(8, 128).T)

    d["bq_in"], d["bk_in"], d["bo_in"] = bvec(bq), bvec(bk), bvec(bo)
    d["bv_in"] = np.ascontiguousarray(bv.reshape(1, 1024))
    d["cw_in"] = np.ascontiguousarray(
        cw[:, 0, :].T.reshape(3, 32, 128).transpose(0, 2, 1))
    d["cb_in"] = np.ascontiguousarray(cb.reshape(32, 128).T)
    return d


_CACHED = {}


def kernel(L0_attn_chunk, L0_attn_Wq, L0_attn_bq, L0_attn_Wk, L0_attn_bk,
           L0_attn_Wv, L0_attn_bv, L0_attn_Wo, L0_attn_bo,
           L0_ffn_w1, L0_ffn_conv_w, L0_ffn_conv_b, L0_ffn_w2):
    cfg = Cfg(nseq=2, tseq=4096)
    n_cores = 8

    x = np.asarray(L0_attn_chunk, dtype=np.float32)
    B, T, D = x.shape
    x_fm = np.ascontiguousarray(x.reshape(B, T, 8, 128).transpose(0, 3, 2, 1))

    wd = _prep_weights(
        np.asarray(L0_attn_Wq, np.float32), np.asarray(L0_attn_bq, np.float32),
        np.asarray(L0_attn_Wk, np.float32), np.asarray(L0_attn_bk, np.float32),
        np.asarray(L0_attn_Wv, np.float32), np.asarray(L0_attn_bv, np.float32),
        np.asarray(L0_attn_Wo, np.float32), np.asarray(L0_attn_bo, np.float32),
        np.asarray(L0_ffn_w1, np.float32), np.asarray(L0_ffn_conv_w, np.float32),
        np.asarray(L0_ffn_conv_b, np.float32), np.asarray(L0_ffn_w2, np.float32))
    idf, idb, mask, onesb = _prep_consts()
    wd.update(dict(idf_in=idf, idb_in=idb, mask_in=mask, onesb_in=onesb))

    if "nc" not in _CACHED:
        _CACHED["nc"] = build_program(cfg)
    nc = _CACHED["nc"]

    in_maps = []
    for ci in range(n_cores):
        m = dict(wd)
        m["x_in"] = np.ascontiguousarray(x_fm[ci * 2:(ci + 1) * 2])
        in_maps.append(m)

    res = run_bass_kernel_spmd(nc, in_maps, core_ids=list(range(n_cores)))
    outs = [r["y_out"] for r in res.results]
    y = np.concatenate(outs, axis=0)
    y = np.ascontiguousarray(
        y.transpose(0, 3, 2, 1).reshape(B, T, D).astype(np.float32))
    return y


# version 03 (final)
# speedup vs baseline: 1.0768x; 1.0731x over previous
import contextlib
import numpy as np
import ml_dtypes

import concourse.bass as bass
import concourse.mybir as mybir
from concourse.bass_utils import run_bass_kernel_spmd

import concourse.bass_utils as _bu
_orig_run_command = _bu.run_command
def _fast_run_command(cmd, **kw):
    import os as _os
    if not _os.environ.get("KM_BIRSIM"):
        cmd = [c.replace("--enable-birsim=true", "--enable-birsim=false")
               if isinstance(c, str) else c for c in cmd]
    return _orig_run_command(cmd, **kw)
_bu.run_command = _fast_run_command

F32 = mybir.dt.float32
BF16 = mybir.dt.bfloat16
I8 = mybir.dt.int8
AF = mybir.ActivationFunctionType
ALU = mybir.AluOpType
AX = mybir.AxisListType

ENGS = ("T", "V", "S", "G", "Y")


class Cfg:
    def __init__(self, nseq=2, tseq=4096):
        self.NSEQ = nseq
        self.TSEQ = tseq
        self.D = 1024
        self.H = 4096
        self.DC = self.D // 128
        self.HC = self.H // 128
        self.CH = 128
        self.TS = 256
        self.CPS = self.TS // self.CH
        self.NSUP = self.TSEQ // self.TS
        self.EPS_LN = 1e-5
        self.EPS_AT = 1e-6


class Builder:
    def __init__(self, nc):
        self.nc = nc
        self.Q = {e: [] for e in ENGS}
        self.sems = {}
        self.cnt = {}
        self.stack = contextlib.ExitStack()
        self.res_free = {}

    def sem(self, name):
        self.sems[name] = self.stack.enter_context(self.nc.semaphore(name))
        self.cnt[name] = 0

    def sb(self, name, shape, dtype=F32):
        return self.stack.enter_context(self.nc.sbuf_tensor(name, shape, dtype))

    def ps(self, name, shape, dtype=F32):
        return self.stack.enter_context(self.nc.psum_tensor(name, shape, dtype))

    def em(self, eng, fn, inc=None, n=1):
        if inc is not None:
            h = self.sems[inc]
            self.cnt[inc] += n
            self.Q[eng].append(lambda e, fn=fn, h=h, n=n: fn(e).then_inc(h, n))
        else:
            self.Q[eng].append(lambda e, fn=fn: fn(e))

    def wt(self, eng, sem, val=None):
        v = self.cnt[sem] if val is None else val
        h = self.sems[sem]
        self.Q[eng].append(lambda e, h=h, v=v: e.wait_ge(h, v))

    def snap(self, sem):
        return self.cnt[sem]

    def flush(self, eng):
        self.wt(eng, eng)

    def claim(self, eng, name):
        ent = self.res_free.get(name)
        if ent is not None:
            for sem, v in ent:
                self.wt(eng, sem, v)

    def release(self, name, *sems):
        self.res_free[name] = [(s, self.cnt[s]) for s in sems]


def build_program(cfg: Cfg, stage: int = 3):
    nc = bass.Bass("TRN2", target_bir_lowering=False, debug=False,
                   detect_race_conditions=False)
    c = cfg
    NSUP_T = c.NSEQ * c.NSUP
    TSH = c.TS + 2

    x_in = nc.dram_tensor("x_in", [c.NSEQ, 128, c.DC, c.TSEQ], F32, kind="ExternalInput").ap()
    y_out = nc.dram_tensor("y_out", [c.NSEQ, 128, c.DC, c.TSEQ], F32, kind="ExternalOutput").ap()
    wq_in = nc.dram_tensor("wq_in", [128, c.DC, c.D], F32, kind="ExternalInput").ap()
    wk_in = nc.dram_tensor("wk_in", [128, c.DC, c.D], F32, kind="ExternalInput").ap()
    wv_in = nc.dram_tensor("wv_in", [128, c.DC, c.D], F32, kind="ExternalInput").ap()
    wo_in = nc.dram_tensor("wo_in", [128, c.DC, c.D], F32, kind="ExternalInput").ap()
    w1_in = nc.dram_tensor("w1_in", [128, c.DC, c.H], F32, kind="ExternalInput").ap()
    w2_in = nc.dram_tensor("w2_in", [128, c.HC, c.D], F32, kind="ExternalInput").ap()
    bq_in = nc.dram_tensor("bq_in", [128, c.DC], F32, kind="ExternalInput").ap()
    bk_in = nc.dram_tensor("bk_in", [128, c.DC], F32, kind="ExternalInput").ap()
    bv_in = nc.dram_tensor("bv_in", [1, c.D], F32, kind="ExternalInput").ap()
    bo_in = nc.dram_tensor("bo_in", [128, c.DC], F32, kind="ExternalInput").ap()
    cw_in = nc.dram_tensor("cw_in", [3, 128, c.HC], F32, kind="ExternalInput").ap()
    cb_in = nc.dram_tensor("cb_in", [128, c.HC], F32, kind="ExternalInput").ap()
    idf_in = nc.dram_tensor("idf_in", [128, 128], F32, kind="ExternalInput").ap()
    idb_in = nc.dram_tensor("idb_in", [128, 128], BF16, kind="ExternalInput").ap()
    mask_in = nc.dram_tensor("mask_in", [128, 128], F32, kind="ExternalInput").ap()
    onesb_in = nc.dram_tensor("onesb_in", [128, 1], BF16, kind="ExternalInput").ap()

    wq_s = nc.dram_tensor("wq_s", [128, c.DC, c.D], BF16, kind="Internal").ap()
    wk_s = nc.dram_tensor("wk_s", [128, c.DC, c.D], BF16, kind="Internal").ap()
    wv_s = nc.dram_tensor("wv_s", [128, c.DC, c.D], BF16, kind="Internal").ap()
    wo_s = nc.dram_tensor("wo_s", [128, c.DC, c.D], BF16, kind="Internal").ap()
    w1q_s = nc.dram_tensor("w1q_s", [c.HC, 128, c.D], BF16, kind="Internal").ap()
    w2q_s = nc.dram_tensor("w2q_s", [c.DC, 128, c.H], BF16, kind="Internal").ap()
    NROWK = 8
    rows_s = nc.dram_tensor("rows_s", [NROWK, 2, NSUP_T, c.TS + 4], F32, kind="Internal").ap()
    RK_FA, RK_FB, RK_FRS, RK_FS2, RK_FSS2, RK_AA, RK_AB, RK_RD = range(8)

    b = Builder(nc)
    with b.stack:
        for s in ["T", "V", "S", "G"]:
            b.sem(s)
        for i in range(3):
            b.sem(f"sx{i}")
            b.sem(f"so{i}")
            b.sem(f"sw1{i}")
        for i in range(2):
            b.sem(f"sw2{i}")
        b.sem("swa0")
        b.sem("swa1")
        b.sem("srow")
        b.sem("sbc")
        b.sem("scu")
        b.sem("sch")
        b.sem("sck")
        b.sem("spro")

        ring = [b.sb(f"ring{i}", [128, c.DC, TSH], F32) for i in range(3)]
        hr = b.sb("hr", [128, c.HC, TSH], F32)
        hrf = hr[:].rearrange("p a t -> p (a t)")
        uq8s = [b.sb(f"uq8_{i}", [128, c.DC, TSH], I8) for i in range(2)]
        uqbs = [b.sb(f"uqb_{i}", [128, c.DC, TSH], BF16) for i in range(2)]
        tmpUd = b.sb("tmpUd", [128, c.DC, TSH], F32)
        tmpUf = tmpUd[:].rearrange("p a t -> p (a t)")
        hq8 = b.sb("hq8", [128, c.HC, c.TS], I8)
        hq8f = hq8[:].rearrange("p a t -> p (a t)")
        kvq = b.sb("kvq", [128, 8192], BF16)
        kvb3 = kvq[:].rearrange("p (a t) -> p a t", a=c.DC)
        hqb3 = kvq[:].rearrange("p (a t) -> p a t", a=c.HC)
        w1sl = [b.sb(f"w1sl{i}", [128, c.DC, 128], BF16) for i in range(3)]
        w2sl = [b.sb(f"w2sl{i}", [128, c.HC, 128], BF16) for i in range(2)]
        wasl = [b.sb(f"wasl{i}", [128, c.DC, 256], BF16) for i in range(2)]
        kv = b.sb("kv", [128, c.DC, c.D], F32)
        ks = b.sb("ks", [128, c.DC], F32)
        ksb = b.sb("ksb", [128, c.DC], BF16)
        qf = b.sb("qf", [128, c.DC, c.TS], BF16)
        kf = b.sb("kf", [128, c.DC, c.TS], BF16)
        ktm = b.sb("ktm", [128, c.CPS, c.D], BF16)
        vtm = b.sb("vtm", [128, c.CPS, c.D], BF16)
        uA = b.sb("uA", [128, c.DC, c.TS], BF16)
        afm = uA
        AT = b.sb("ATt", [128, 3, 128], BF16)
        bcFAs = [b.sb(f"bcFA{i}", [128, TSH], F32) for i in range(2)]
        bcFBs = [b.sb(f"bcFB{i}", [128, TSH], F32) for i in range(2)]
        bcFRSs = [b.sb(f"bcFRS{i}", [128, TSH], F32) for i in range(2)]
        bcFS2 = b.sb("bcFS2", [128, c.TS], F32)
        bcFSS2 = b.sb("bcFSS2", [128, c.TS], F32)
        bcAA = b.sb("bcAA", [128, c.TS], F32)
        bcAB = b.sb("bcAB", [128, c.TS], F32)
        bcRD = b.sb("bcRD", [128, 128], F32)
        bvbc = b.sb("bvbc", [128, c.D], F32)
        tmpa = b.sb("tmpa", [128, TSH], F32)
        tmpb = b.sb("tmpb", [128, TSH], F32)
        tmpc = b.sb("tmpc", [128, TSH], F32)
        tmpd = b.sb("tmpd", [128, TSH], F32)
        stats_t = [b.sb(f"stat{i}", [128, 18], F32) for i in range(2)]
        rowsts = [b.sb(f"rowst{i}", [128, 21], F32) for i in range(2)]
        kswork = b.sb("kswork", [128, c.DC], F32)
        sliver = [b.sb(f"sliver{i}", [128, c.DC, 1], F32) for i in range(2)]
        convL = b.sb("convL", [128, c.HC], F32)
        convM = b.sb("convM", [128, c.HC], F32)
        convR = b.sb("convR", [128, c.HC], F32)
        convB = b.sb("convB", [128, c.HC], F32)
        batt = {}
        for nm in ["bq", "bk", "bo"]:
            batt[nm] = b.sb(nm, [128, c.DC], F32)
            batt["n" + nm] = b.sb("n" + nm, [128, c.DC], F32)
        idf = b.sb("idf", [128, 128], F32)
        idb = b.sb("idb", [128, 128], BF16)
        maskt = b.sb("maskt", [128, 128], F32)
        onesb = b.sb("onesb", [128, 1], BF16)
        epst = b.sb("epst", [128, 1], F32)
        scal = b.sb("scal", [1, 8], F32)
        scalb = b.sb("scalb", [128, 8], F32)
        den_r = b.sb("den_r", [1, 128], F32)

        P0 = b.ps("P0", [128, 1024], F32)
        P1 = b.ps("P1", [128, 1024], F32)
        P2 = b.ps("P2", [128, TSH], F32)
        P3 = b.ps("P3", [128, TSH], F32)
        P4 = b.ps("P4", [128, TSH], F32)
        PB = b.ps("PBf", [128, 1024], BF16)
        Pm = [P2, P3, P4]
        PmN = ["P2", "P3", "P4"]

        for dst, src in [(idf, idf_in), (idb, idb_in), (maskt, mask_in),
                         (onesb, onesb_in), (convB, cb_in),
                         (batt["bq"], bq_in), (batt["bk"], bk_in), (batt["bo"], bo_in)]:
            b.em("Y", lambda e, d=dst, s=src: e.dma_start(out=d[:], in_=s),
                 inc="spro", n=16)
        for i, t in enumerate([convL, convM, convR]):
            b.em("Y", lambda e, t=t, i=i: e.dma_start(out=t[:], in_=cw_in[i]),
                 inc="spro", n=16)
        b.em("Y", lambda e: e.dma_start(out=bvbc[:], in_=bv_in.partition_broadcast(128)),
             inc="spro", n=16)
        b.wt("V", "spro")
        b.em("V", lambda e: e.memset(epst[:], c.EPS_LN))
        for i in range(2):
            b.em("V", lambda e, i=i: e.memset(rowsts[i][:], 0.0))
            b.em("V", lambda e, i=i: e.memset(stats_t[i][:], 0.0))
        b.em("V", lambda e: e.memset(kswork[:], 0.0))
        b.em("V", lambda e: e.memset(den_r[:], 0.0))
        b.em("V", lambda e: e.memset(tmpUd[:], 0.0))
        b.em("V", lambda e: e.memset(kv[:], 0.0))
        b.em("V", lambda e: e.memset(ks[:], 0.0))
        b.em("V", lambda e: e.memset(ksb[:], 0.0))
        for nm in ["bq", "bk", "bo"]:
            b.em("V", lambda e, s=batt[nm], d=batt["n" + nm]:
                 e.tensor_scalar_mul(out=d[:], in0=s[:], scalar1=-1.0), inc="V")

        QW = 2048
        for wsrc, wdst in [(wq_in, wq_s), (wk_in, wk_s), (wv_in, wv_s), (wo_in, wo_s)]:
            for qtr in range(4):
                sl = slice(qtr * 256, (qtr + 1) * 256)
                b.wt("Y", "V")
                b.em("Y", lambda e, wsrc=wsrc, sl=sl: e.dma_start(
                    out=hrf[:, 0:QW].rearrange("p (a t) -> p a t", a=c.DC),
                    in_=wsrc[:, :, sl]), inc="spro", n=16)
                b.wt("V", "spro")
                b.em("V", lambda e: e.tensor_copy(
                    out=kvq[:, 0:QW], in_=hrf[:, 0:QW]), inc="V")
                b.wt("Y", "V")
                b.em("Y", lambda e, wdst=wdst, sl=sl: e.dma_start(
                    out=wdst[:, :, sl],
                    in_=kvq[:, 0:QW].rearrange("p (a t) -> p a t", a=c.DC)),
                    inc="spro", n=16)
        b.wt("V", "spro")

        b.em("V", lambda e: e.memset(tmpa[:, 0:2], 0.0), inc="V")

        def absacc(wsrc, n_m, m_w, col):
            flat_w = m_w * 128
            for m in range(n_m):
                b.wt("Y", "V")
                b.em("Y", lambda e, m=m, wsrc=wsrc, flat_w=flat_w, m_w=m_w:
                     e.dma_start(out=hrf[:, 0:flat_w].rearrange(
                         "p (a t) -> p a t", a=m_w),
                         in_=wsrc[:, :, m * 128:(m + 1) * 128]), inc="spro", n=16)
                b.wt("V", "spro")
                b.em("V", lambda e, flat_w=flat_w: e.tensor_reduce(
                    out=tmpb[:, 0:1], in_=hrf[:, 0:flat_w],
                    axis=AX.X, op=ALU.add, apply_absolute_value=True), inc="V")
                b.flush("V")
                b.em("V", lambda e, col=col: e.tensor_add(
                    out=tmpa[:, col:col + 1], in0=tmpa[:, col:col + 1],
                    in1=tmpb[:, 0:1]), inc="V")
                b.flush("V")

        absacc(w1_in, c.HC, c.DC, 0)
        absacc(w2_in, c.DC, c.HC, 1)
        b.wt("T", "V")
        b.claim("T", "P2")
        b.em("T", lambda e: e.transpose(out=P2[0:1, 0:128], in_=tmpa[:, 0:1],
                                        identity=idf[:]), inc="T")
        b.em("T", lambda e: e.transpose(out=P2[0:1, 128:256], in_=tmpa[:, 1:2],
                                        identity=idf[:]), inc="T")
        b.wt("V", "T")
        b.em("V", lambda e: e.tensor_reduce(out=scal[0:1, 2:3], in_=P2[0:1, 0:128],
                                            axis=AX.X, op=ALU.add), inc="V")
        b.em("V", lambda e: e.tensor_reduce(out=scal[0:1, 3:4], in_=P2[0:1, 128:256],
                                            axis=AX.X, op=ALU.add), inc="V")
        b.release("P2", "V")
        b.flush("V")

        def final_scale(col_sum, col_s, n_elems, out_col_m):
            b.em("V", lambda e: e.tensor_scalar(
                out=scal[0:1, col_s:col_s + 1], in0=scal[0:1, col_sum:col_sum + 1],
                scalar1=1.0 / n_elems, scalar2=1e-5, op0=ALU.mult, op1=ALU.max),
                inc="V")
            b.flush("V")
            b.em("V", lambda e: e.tensor_copy(
                out=scal[0:1, out_col_m:out_col_m + 1],
                in_=scal[0:1, col_s:col_s + 1]), inc="V")
            b.flush("V")
            b.em("V", lambda e: e.reciprocal(
                out=scal[0:1, col_s:col_s + 1], in_=scal[0:1, col_s:col_s + 1]),
                inc="V")
            b.flush("V")

        final_scale(2, 0, c.D * c.H, 4)
        final_scale(3, 1, c.D * c.H, 5)
        b.em("V", lambda e: e.tensor_scalar_mul(out=scal[0:1, 6:7], in0=scal[0:1, 4:5],
                                                scalar1=1.0 / 127.0), inc="V")
        b.em("V", lambda e: e.tensor_scalar_mul(out=scal[0:1, 7:8], in0=scal[0:1, 5:6],
                                                scalar1=0.5 / 127.0), inc="V")
        b.flush("V")
        b.wt("G", "V")
        b.em("G", lambda e: e.dma_start(out=rows_s[0, 1, 0, 0:8], in_=scal[0:1, 0:8]),
             inc="srow", n=16)
        b.wt("G", "srow")
        b.em("G", lambda e: e.dma_start(
            out=scalb[:], in_=rows_s[0, 1, 0, 0:8].partition_broadcast(128)),
            inc="sbc", n=16)
        b.wt("V", "sbc")

        def ternarize(wsrc, wdst, n_m, m_w, s_col):
            flat_w = m_w * 128
            for m in range(n_m):
                b.wt("Y", "G")
                b.wt("Y", "V")
                b.wt("Y", "spro")
                b.em("Y", lambda e, m=m, wsrc=wsrc, flat_w=flat_w, m_w=m_w:
                     e.dma_start(out=hrf[:, 0:flat_w].rearrange(
                         "p (a t) -> p a t", a=m_w),
                         in_=wsrc[:, :, m * 128:(m + 1) * 128]), inc="spro", n=16)
                b.wt("V", "spro")
                b.em("V", lambda e, flat_w=flat_w, s_col=s_col: e.tensor_scalar(
                    out=hrf[:, 0:flat_w], in0=hrf[:, 0:flat_w],
                    scalar1=scalb[:, s_col:s_col + 1], scalar2=None,
                    op0=ALU.mult), inc="V")
                b.flush("V")
                b.em("V", lambda e, flat_w=flat_w: e.tensor_scalar(
                    out=hrf[:, 0:flat_w], in0=hrf[:, 0:flat_w],
                    scalar1=-1.0, scalar2=1.0, op0=ALU.max, op1=ALU.min), inc="V")
                b.flush("V")
                b.em("V", lambda e, flat_w=flat_w: e.tensor_copy(
                    out=hq8f[:, 0:flat_w], in_=hrf[:, 0:flat_w]), inc="V")
                b.wt("G", "V")
                b.em("G", lambda e, flat_w=flat_w: e.dma_start(
                    out=kvq[:, 0:flat_w], in_=hq8f[:, 0:flat_w]), inc="G", n=16)
                b.wt("Y", "G")
                b.em("Y", lambda e, m=m, wdst=wdst, flat_w=flat_w: e.dma_start(
                    out=wdst[m], in_=kvq[:, 0:flat_w]), inc="spro", n=16)

        ternarize(w1_in, w1q_s, c.HC, c.DC, 0)
        ternarize(w2_in, w2q_s, c.DC, c.HC, 1)
        b.wt("V", "spro")
        b.wt("T", "spro")
        b.wt("G", "spro")

        def stats_rows(seq, sup_i, slot, cols_lo, cols_hi, kinds, is_ffn, sub_idx):
            n_tok = cols_hi - cols_lo
            nt = (n_tok + 127) // 128
            src = ring[slot]
            row_i = seq * c.NSUP + sup_i
            stat = stats_t[sub_idx]
            rowst = rowsts[sub_idx]
            st_ = lambda f: stat[:, f * 3:f * 3 + nt]
            rw_ = lambda j: rowst[:, j * 3:j * 3 + nt]
            b.wt("T", "V")
            b.wt("T", "S")
            for tc in range(nt):
                lo = cols_lo + tc * 128
                hi = min(cols_lo + (tc + 1) * 128, cols_hi)
                w = hi - lo
                P = P0 if tc % 2 == 0 else P1
                nmP = "P0" if tc % 2 == 0 else "P1"
                rgn = (tc % 2) * 1024
                b.claim("T", nmP)
                for fc in range(c.DC):
                    b.em("T", lambda e, P=P, fc=fc, lo=lo, hi=hi, w=w, src=src:
                         e.transpose(out=P[0:w, fc * 128:(fc + 1) * 128],
                                     in_=src[:, fc, lo:hi], identity=idf[:]),
                         inc="T" if fc == c.DC - 1 else None)
                tT = b.snap("T")
                b.wt("V", "T", tT)
                b.wt("S", "T", tT)
                b.em("V", lambda e, P=P, w=w, tc=tc, stat=stat: e.tensor_reduce(
                    out=stat[0:w, tc:tc + 1], in_=P[0:w, :], axis=AX.X,
                    op=ALU.add), inc="V")
                b.wt("S", "V")
                b.em("S", lambda e, P=P, w=w, rgn=rgn: e.activation(
                    out=tmpUf[0:w, rgn:rgn + 1024], in_=P[0:w, :],
                    func=AF.Square), inc="S")
                b.wt("V", "S")
                b.em("V", lambda e, w=w, tc=tc, rgn=rgn, stat=stat: e.tensor_reduce(
                    out=stat[0:w, 3 + tc:4 + tc], in_=tmpUf[0:w, rgn:rgn + 1024],
                    axis=AX.X, op=ALU.add), inc="V")
                if is_ffn:
                    b.em("V", lambda e, P=P, w=w, tc=tc, stat=stat: e.tensor_reduce(
                        out=stat[0:w, 6 + tc:7 + tc], in_=P[0:w, :], axis=AX.X,
                        op=ALU.max), inc="V")
                    b.em("V", lambda e, P=P, w=w, tc=tc, stat=stat: e.tensor_reduce(
                        out=stat[0:w, 9 + tc:10 + tc], in_=P[0:w, :], axis=AX.X,
                        op=ALU.min), inc="V")
                b.release(nmP, "V", "S")
            b.flush("V")
            b.claim("V", f"rowst{sub_idx}")
            b.em("V", lambda e: e.tensor_scalar_mul(
                out=st_(4), in0=st_(0), scalar1=1.0 / c.D), inc="V")
            b.flush("V")
            b.em("V", lambda e: e.tensor_mul(
                out=rw_(0), in0=st_(4), in1=st_(4)), inc="V")
            b.flush("V")
            b.em("V", lambda e: e.scalar_tensor_tensor(
                out=st_(5), in0=st_(1), scalar=1.0 / c.D,
                in1=rw_(0), op0=ALU.mult, op1=ALU.subtract), inc="V")
            b.flush("V")
            b.wt("S", "V")
            b.em("S", lambda e: e.activation(
                out=st_(5), in_=st_(5), func=AF.Sqrt,
                bias=epst[:, 0:1], scale=1.0), inc="S")
            b.wt("V", "S")
            b.em("V", lambda e: e.reciprocal(out=st_(5), in_=st_(5)), inc="V")
            b.flush("V")
            if is_ffn:
                b.em("V", lambda e: e.tensor_sub(
                    out=rw_(1), in0=st_(2), in1=st_(4)), inc="V")
                b.em("V", lambda e: e.tensor_sub(
                    out=rw_(2), in0=st_(4), in1=st_(3)), inc="V")
                b.flush("V")
                b.em("V", lambda e: e.tensor_max(
                    out=rw_(1), in0=rw_(1), in1=rw_(2)), inc="V")
                b.flush("V")
                b.em("V", lambda e: e.tensor_mul(
                    out=rw_(1), in0=rw_(1), in1=st_(5)), inc="V")
                b.flush("V")
                b.em("V", lambda e: e.tensor_scalar_max(
                    out=rw_(1), in0=rw_(1), scalar1=1e-5), inc="V")
                b.flush("V")
                b.em("V", lambda e: e.reciprocal(out=rw_(3), in_=rw_(1)), inc="V")
                b.flush("V")
                b.em("V", lambda e: e.tensor_scalar_mul(
                    out=rw_(3), in0=rw_(3), scalar1=127.0), inc="V")
                b.flush("V")
                b.em("V", lambda e: e.tensor_scalar(
                    out=rw_(4), in0=rw_(1),
                    scalar1=scalb[:, 6:7], scalar2=None, op0=ALU.mult), inc="V")
                b.em("V", lambda e: e.tensor_mul(
                    out=rw_(5), in0=st_(5), in1=rw_(3)), inc="V")
                b.flush("V")
                b.em("V", lambda e: e.scalar_tensor_tensor(
                    out=rw_(6), in0=st_(4), scalar=-1.0,
                    in1=rw_(5), op0=ALU.mult, op1=ALU.mult), inc="V")
                b.flush("V")
                emitted = [(RK_FA, 5), (RK_FB, 6), (RK_FRS, 4)]
            else:
                b.em("V", lambda e: e.tensor_copy(out=rw_(5), in_=st_(5)), inc="V")
                b.flush("V")
                b.em("V", lambda e: e.scalar_tensor_tensor(
                    out=rw_(6), in0=st_(4), scalar=-1.0,
                    in1=rw_(5), op0=ALU.mult, op1=ALU.mult), inc="V")
                b.flush("V")
                emitted = [(RK_AA, 5), (RK_AB, 6)]
            b.wt("G", "V")
            for tc in range(nt):
                w = min(cols_lo + (tc + 1) * 128, cols_hi) - (cols_lo + tc * 128)
                for rk, colj in emitted:
                    b.em("G", lambda e, rk=rk, colj=colj, w=w, tc=tc, row_i=row_i,
                         sub_idx=sub_idx, rowst=rowst: e.dma_start(
                        out=rows_s[rk, sub_idx, row_i, tc * 128:tc * 128 + w],
                        in_=rowst[0:w, colj * 3 + tc:colj * 3 + tc + 1]),
                        inc="srow", n=16)
            b.release(f"rowst{sub_idx}", "srow")
            b.wt("G", "srow")
            for rk, tile, wdt in kinds:
                b.em("G", lambda e, rk=rk, tile=tile, wdt=wdt, row_i=row_i,
                     sub_idx=sub_idx: e.dma_start(
                    out=tile[:, 0:wdt],
                    in_=rows_s[rk, sub_idx, row_i, 0:wdt].partition_broadcast(128)),
                    inc="sbc", n=16)

        front_state = {}

        def ffn_front(seq, sup_i, slot, sub_idx, first_sup, last_sup):
            uq8 = uq8s[sub_idx]
            uqb = uqbs[sub_idx]
            b.wt("T", f"sx{slot}")
            b.wt("V", f"sx{slot}")
            stats_rows(seq, sup_i, slot, 0, TSH,
                       [(RK_FA, bcFAs[sub_idx], TSH), (RK_FB, bcFBs[sub_idx], TSH),
                        (RK_FRS, bcFRSs[sub_idx], TSH)], True, sub_idx)
            vbc = b.snap("sbc")
            b.wt("V", "sbc", vbc)
            b.claim("V", f"uq8buf{sub_idx}")
            b.em("V", lambda e, slot=slot, sub_idx=sub_idx: e.tensor_tensor(
                out=tmpUd[:], in0=ring[slot][:],
                in1=bcFAs[sub_idx][:, 0:TSH].unsqueeze(1)
                    .broadcast_to([128, c.DC, TSH]),
                op=ALU.mult), inc="V")
            b.flush("V")
            b.em("V", lambda e, uq8=uq8, sub_idx=sub_idx: e.tensor_tensor(
                out=uq8[:], in0=tmpUd[:],
                in1=bcFBs[sub_idx][:, 0:TSH].unsqueeze(1)
                    .broadcast_to([128, c.DC, TSH]),
                op=ALU.add), inc="V")
            if first_sup:
                b.em("V", lambda e, uq8=uq8: e.memset(uq8[:, :, 0:1], 0), inc="V")
            if last_sup:
                b.em("V", lambda e, uq8=uq8: e.memset(uq8[:, :, TSH - 1:TSH], 0),
                     inc="V")
            b.wt("G", "V")
            b.claim("G", f"uqbbuf{sub_idx}")
            b.em("G", lambda e, uqb=uqb, uq8=uq8: e.dma_start(out=uqb[:], in_=uq8[:]),
                 inc="scu", n=16)
            b.release(f"uq8buf{sub_idx}", "scu")
            front_state[(seq, sup_i, sub_idx)] = (vbc, b.snap("scu"))

        def ffn_back(seq, sup_i, slot, sub_idx):
            uqb = uqbs[sub_idx]
            bcFRS = bcFRSs[sub_idx]
            vbc, scu_t = front_state.pop((seq, sup_i, sub_idx))
            row_i = seq * c.NSUP + sup_i
            for m in range(c.HC):
                wsl = w1sl[m % 3]
                wsem = f"sw1{m % 3}"
                b.claim("Y", f"w1c{m % 3}")
                b.em("Y", lambda e, m=m, wsl=wsl: e.dma_start(
                    out=wsl[:].rearrange("p a t -> p (a t)"), in_=w1q_s[m]),
                    inc=wsem, n=16)
                Pn = PmN[m % 3]
                P = Pm[m % 3]
                b.claim("T", Pn)
                b.wt("T", wsem)
                if m == 0:
                    b.wt("T", "scu", scu_t)
                for k in range(c.DC):
                    b.em("T", lambda e, P=P, k=k, wsl=wsl, uqb=uqb: e.matmul(
                        out=P[:, 0:TSH], lhsT=wsl[:, k, :], rhs=uqb[:, k, :],
                        start=(k == 0), stop=(k == c.DC - 1)),
                        inc="T" if k == c.DC - 1 else None)
                b.release(f"w1c{m % 3}", "T")
                if m == c.HC - 1:
                    b.release(f"uqbbuf{sub_idx}", "T")
                b.wt("V", "T")
                if m == 0:
                    b.wt("V", "sbc", vbc)
                b.em("V", lambda e, P=P, m=m, bcFRS=bcFRS: e.tensor_mul(
                    out=hr[:, m, :], in0=P[:, 0:TSH], in1=bcFRS[:, 0:TSH]), inc="V")
                b.release(Pn, "V")
                b.wt("S", "V")
                ct = tmpb if m % 2 == 0 else tmpc
                b.em("S", lambda e, m=m, ct=ct: e.activation(
                    out=ct[:, 0:c.TS], in_=hr[:, m, 1:TSH - 1], func=AF.Identity,
                    bias=convB[:, m:m + 1], scale=convM[:, m:m + 1]), inc="S")
                b.wt("V", "S")
                b.em("V", lambda e, m=m, ct=ct: e.scalar_tensor_tensor(
                    out=ct[:, 0:c.TS], in0=hr[:, m, 0:TSH - 2],
                    scalar=convL[:, m:m + 1], in1=ct[:, 0:c.TS],
                    op0=ALU.mult, op1=ALU.add), inc="V")
                b.flush("V")
                b.em("V", lambda e, m=m, ct=ct: e.scalar_tensor_tensor(
                    out=ct[:, 0:c.TS], in0=hr[:, m, 2:TSH],
                    scalar=convR[:, m:m + 1], in1=ct[:, 0:c.TS],
                    op0=ALU.mult, op1=ALU.add), inc="V")
                b.wt("S", "V")
                b.em("S", lambda e, m=m, ct=ct: e.activation(
                    out=hr[:, m, 0:c.TS], in_=ct[:, 0:c.TS], func=AF.Silu), inc="S")
            b.wt("V", "S")
            b.em("V", lambda e: e.tensor_reduce(
                out=tmpa[:, 0:c.TS], in_=hr[:, :, 0:c.TS].transpose([0, 2, 1]),
                axis=AX.X, op=ALU.max, apply_absolute_value=True), inc="V")
            b.wt("T", "V")
            b.claim("T", "P0")
            for tc in range(c.CPS):
                b.em("T", lambda e, tc=tc: e.transpose(
                    out=P0[:, tc * 128:(tc + 1) * 128],
                    in_=tmpa[:, tc * 128:(tc + 1) * 128], identity=idf[:]),
                    inc="T" if tc == c.CPS - 1 else None)
            b.wt("V", "T")
            rowst = rowsts[sub_idx]
            b.claim("V", f"rowst{sub_idx}")
            nt = c.CPS
            rw_ = lambda j: rowst[:, j * 3:j * 3 + nt]
            for tc in range(c.CPS):
                b.em("V", lambda e, tc=tc, rowst=rowst: e.tensor_reduce(
                    out=rowst[:, tc:tc + 1], in_=P0[:, tc * 128:(tc + 1) * 128],
                    axis=AX.X, op=ALU.max), inc="V")
            b.release("P0", "V")
            b.flush("V")
            b.em("V", lambda e, rw_=rw_: e.tensor_scalar_max(
                out=rw_(0), in0=rw_(0), scalar1=1e-5), inc="V")
            b.flush("V")
            b.em("V", lambda e, rw_=rw_: e.reciprocal(out=rw_(1), in_=rw_(0)),
                 inc="V")
            b.flush("V")
            b.em("V", lambda e, rw_=rw_: e.tensor_scalar_mul(
                out=rw_(1), in0=rw_(1), scalar1=127.0), inc="V")
            b.em("V", lambda e, rw_=rw_: e.tensor_scalar(
                out=rw_(2), in0=rw_(0),
                scalar1=scalb[:, 7:8], scalar2=None, op0=ALU.mult), inc="V")
            b.flush("V")
            b.wt("G", "V")
            for tc in range(c.CPS):
                b.em("G", lambda e, tc=tc, row_i=row_i, sub_idx=sub_idx,
                     rowst=rowst: e.dma_start(
                    out=rows_s[RK_FS2, sub_idx, row_i, tc * 128:(tc + 1) * 128],
                    in_=rowst[:, 3 + tc:4 + tc]), inc="srow", n=16)
                b.em("G", lambda e, tc=tc, row_i=row_i, sub_idx=sub_idx,
                     rowst=rowst: e.dma_start(
                    out=rows_s[RK_FSS2, sub_idx, row_i, tc * 128:(tc + 1) * 128],
                    in_=rowst[:, 6 + tc:7 + tc]), inc="srow", n=16)
            b.release(f"rowst{sub_idx}", "srow")
            b.wt("G", "srow")
            b.em("G", lambda e, row_i=row_i, sub_idx=sub_idx: e.dma_start(
                out=bcFS2[:, 0:c.TS],
                in_=rows_s[RK_FS2, sub_idx, row_i, 0:c.TS].partition_broadcast(128)),
                inc="sbc", n=16)
            b.em("G", lambda e, row_i=row_i, sub_idx=sub_idx: e.dma_start(
                out=bcFSS2[:, 0:c.TS],
                in_=rows_s[RK_FSS2, sub_idx, row_i, 0:c.TS].partition_broadcast(128)),
                inc="sbc", n=16)
            b.wt("V", "sbc")
            b.claim("V", "hq8buf")
            b.em("V", lambda e: e.tensor_tensor(
                out=hq8[:], in0=hr[:, :, 0:c.TS],
                in1=bcFS2[:, 0:c.TS].unsqueeze(1).broadcast_to([128, c.HC, c.TS]),
                op=ALU.mult), inc="V")
            vhq = b.snap("V")
            b.claim("G", "kvqbuf")
            b.wt("G", "V", vhq)
            b.em("G", lambda e: e.dma_start(out=hqb3[:], in_=hq8[:]), inc="sch",
                 n=16)
            b.release("hq8buf", "sch")
            sch_t = b.snap("sch")
            for m in range(c.DC):
                wsl = w2sl[m % 2]
                wsem = f"sw2{m % 2}"
                b.claim("Y", f"w2c{m % 2}")
                b.em("Y", lambda e, m=m, wsl=wsl: e.dma_start(
                    out=wsl[:].rearrange("p a t -> p (a t)"), in_=w2q_s[m]),
                    inc=wsem, n=16)
                Pn = "P0" if m % 2 == 0 else "P1"
                P = P0 if m % 2 == 0 else P1
                b.claim("T", Pn)
                b.wt("T", wsem)
                if m == 0:
                    b.wt("T", "sch", sch_t)
                for k in range(c.HC):
                    b.em("T", lambda e, P=P, k=k, wsl=wsl: e.matmul(
                        out=P[:, 0:c.TS], lhsT=wsl[:, k, :], rhs=hqb3[:, k, :],
                        start=(k == 0), stop=(k == c.HC - 1)),
                        inc="T" if k == c.HC - 1 else None)
                b.release(f"w2c{m % 2}", "T")
                if m == c.DC - 1:
                    b.release("kvqbuf", "T")
                b.wt("V", "T")
                ct = tmpc if m % 2 == 0 else tmpd
                b.em("V", lambda e, P=P, ct=ct: e.tensor_mul(
                    out=ct[:, 0:c.TS], in0=P[:, 0:c.TS], in1=bcFSS2[:, 0:c.TS]),
                    inc="V")
                b.release(Pn, "V")
                b.em("V", lambda e, m=m, ct=ct, slot=slot: e.tensor_add(
                    out=ring[slot][:, m, 1:TSH - 1],
                    in0=ring[slot][:, m, 1:TSH - 1], in1=ct[:, 0:c.TS]), inc="V")

        def attn(seq, sup_i, slot):
            row_i = seq * c.NSUP + sup_i
            stats_rows(seq, sup_i, slot, 1, TSH - 1,
                       [(RK_AA, bcAA, c.TS), (RK_AB, bcAB, c.TS)], False, 0)
            b.wt("V", "sbc")
            b.em("V", lambda e, slot=slot: e.tensor_tensor(
                out=tmpUd[:, :, 0:c.TS], in0=ring[slot][:, :, 1:TSH - 1],
                in1=bcAA[:, 0:c.TS].unsqueeze(1).broadcast_to([128, c.DC, c.TS]),
                op=ALU.mult), inc="V")
            b.flush("V")
            b.em("V", lambda e: e.tensor_tensor(
                out=uA[:], in0=tmpUd[:, :, 0:c.TS],
                in1=bcAB[:, 0:c.TS].unsqueeze(1).broadcast_to([128, c.DC, c.TS]),
                op=ALU.add), inc="V")
            vuA = b.snap("V")
            for (wsrc, dst, bt, nbt) in [(wq_s, qf, batt["bq"], batt["nbq"]),
                                         (wk_s, kf, batt["bk"], batt["nbk"])]:
                for qt in range(4):
                    ws = wasl[qt % 2]
                    wsem = f"swa{qt % 2}"
                    b.claim("Y", f"wac{qt % 2}")
                    b.em("Y", lambda e, wsrc=wsrc, qt=qt, ws=ws: e.dma_start(
                        out=ws[:], in_=wsrc[:, :, qt * 256:(qt + 1) * 256]),
                        inc=wsem, n=16)
                    b.wt("T", wsem)
                    b.wt("T", "V", vuA)
                    for ml in range(2):
                        m = qt * 2 + ml
                        Pn = "P2" if m % 2 == 0 else "P3"
                        P = P2 if m % 2 == 0 else P3
                        b.claim("T", Pn)
                        for k in range(c.DC):
                            b.em("T", lambda e, P=P, k=k, ml=ml, ws=ws: e.matmul(
                                out=P[:, 0:c.TS],
                                lhsT=ws[:, k, ml * 128:(ml + 1) * 128],
                                rhs=uA[:, k, :], start=(k == 0),
                                stop=(k == c.DC - 1)),
                                inc="T" if k == c.DC - 1 else None)
                        if ml == 1:
                            b.release(f"wac{qt % 2}", "T")
                        rt = tmpb if m % 2 == 0 else tmpa
                        et = tmpc if m % 2 == 0 else tmpd
                        b.wt("S", "T")
                        b.em("S", lambda e, P=P, m=m, bt=bt, rt=rt: e.activation(
                            out=rt[:, 0:c.TS], in_=P[:, 0:c.TS], func=AF.Relu,
                            bias=bt[:, m:m + 1], scale=1.0), inc="S")
                        b.em("S", lambda e, P=P, m=m, nbt=nbt, et=et: e.activation(
                            out=et[:, 0:c.TS], in_=P[:, 0:c.TS], func=AF.Relu,
                            bias=nbt[:, m:m + 1], scale=-1.0), inc="S")
                        b.release(Pn, "S")
                        b.wt("V", "S")
                        b.em("V", lambda e, et=et: e.tensor_scalar_mul(
                            out=et[:, 0:c.TS], in0=et[:, 0:c.TS], scalar1=-1.0),
                            inc="V")
                        b.wt("S", "V")
                        b.em("S", lambda e, et=et: e.activation(
                            out=et[:, 0:c.TS], in_=et[:, 0:c.TS], func=AF.Exp,
                            scale=1.0), inc="S")
                        b.wt("V", "S")
                        b.em("V", lambda e, dst=dst, m=m, rt=rt, et=et: e.tensor_add(
                            out=dst[:, m, :], in0=rt[:, 0:c.TS],
                            in1=et[:, 0:c.TS]), inc="V")
            for qt in range(4):
                ws = wasl[qt % 2]
                wsem = f"swa{qt % 2}"
                b.claim("Y", f"wac{qt % 2}")
                b.em("Y", lambda e, qt=qt, ws=ws: e.dma_start(
                    out=ws[:], in_=wv_s[:, :, qt * 256:(qt + 1) * 256]),
                    inc=wsem, n=16)
                b.wt("T", wsem)
                b.wt("T", "V", vuA)
                for tc in range(c.CPS):
                    b.claim("T", "P0")
                    for k in range(c.DC):
                        b.em("T", lambda e, tc=tc, k=k, ws=ws: e.matmul(
                            out=P0[:, 0:256],
                            lhsT=uA[:, k, tc * 128:(tc + 1) * 128],
                            rhs=ws[:, k, :], start=(k == 0),
                            stop=(k == c.DC - 1)),
                            inc="T" if k == c.DC - 1 else None)
                    if tc == c.CPS - 1:
                        b.release(f"wac{qt % 2}", "T")
                    b.wt("V", "T")
                    b.em("V", lambda e, tc=tc, qt=qt: e.tensor_add(
                        out=vtm[:, tc, qt * 256:(qt + 1) * 256], in0=P0[:, 0:256],
                        in1=bvbc[:, qt * 256:(qt + 1) * 256]), inc="V")
                    b.release("P0", "V")
            b.wt("T", "V")
            b.claim("T", "PBf")
            for tc in range(c.CPS):
                for fc in range(c.DC):
                    b.em("T", lambda e, tc=tc, fc=fc: e.transpose(
                        out=PB[:, fc * 128:(fc + 1) * 128],
                        in_=kf[:, fc, tc * 128:(tc + 1) * 128], identity=idb[:]),
                        inc="T" if fc == c.DC - 1 else None)
                b.wt("S", "T")
                b.em("S", lambda e, tc=tc: e.copy(out=ktm[:, tc, :], in_=PB[:, 0:c.D]),
                     inc="S")
                if tc < c.CPS - 1:
                    b.wt("T", "S")
            b.release("PBf", "S")
            b.claim("G", "kvqbuf")
            b.em("G", lambda e: e.dma_start(out=kvb3[:], in_=kv[:]), inc="sck", n=16)
            sck_t = b.snap("sck")
            ai_base = 0
            for ch in range(c.CPS):
                ccols = slice(ch * 128, (ch + 1) * 128)
                at_idx = {}
                for sb in range(ch + 1):
                    ai = ai_base + sb
                    at_idx[sb] = ai
                    b.claim("T", "P4")
                    for k in range(c.DC):
                        b.em("T", lambda e, sb=sb, k=k, ccols=ccols: e.matmul(
                            out=P4[:, 0:128], lhsT=kf[:, k, sb * 128:(sb + 1) * 128],
                            rhs=qf[:, k, ccols], start=(k == 0),
                            stop=(k == c.DC - 1)),
                            inc="T" if k == c.DC - 1 else None)
                    if sb == ch:
                        b.wt("V", "T")
                        b.em("V", lambda e, ai=ai: e.tensor_mul(
                            out=AT[:, ai, :], in0=P4[:, 0:128], in1=maskt[:]),
                            inc="V")
                        b.release("P4", "V")
                    else:
                        b.wt("S", "T")
                        b.em("S", lambda e, ai=ai: e.copy(
                            out=AT[:, ai, :], in_=P4[:, 0:128]), inc="S")
                        b.release("P4", "S")
                ai_base += ch + 1
                b.wt("T", "V")
                b.wt("T", "S")
                b.claim("T", "P2")
                nmm = (ch + 1) + c.DC
                i_mm = 0
                for sb in range(ch + 1):
                    i_mm += 1
                    b.em("T", lambda e, sb=sb, at_idx=at_idx, i_mm=i_mm, nmm=nmm:
                         e.matmul(out=P2[0:1, 0:128], lhsT=onesb[:],
                                  rhs=AT[:, at_idx[sb], :],
                                  start=(i_mm == 1), stop=(i_mm == nmm)))
                for k in range(c.DC):
                    i_mm += 1
                    b.em("T", lambda e, k=k, ccols=ccols, i_mm=i_mm, nmm=nmm:
                         e.matmul(out=P2[0:1, 0:128], lhsT=ksb[:, k:k + 1],
                                  rhs=qf[:, k, ccols],
                                  start=(i_mm == 1), stop=(i_mm == nmm)),
                         inc="T" if i_mm == nmm else None)
                b.wt("V", "T")
                b.em("V", lambda e: e.tensor_scalar_add(
                    out=den_r[:, :], in0=P2[0:1, 0:128], scalar1=c.EPS_AT), inc="V")
                b.flush("V")
                b.em("V", lambda e: e.reciprocal(out=den_r[:, :], in_=den_r[:, :]),
                     inc="V")
                b.flush("V")
                b.release("P2", "V")
                b.wt("G", "V")
                b.em("G", lambda e, row_i=row_i, ch=ch: e.dma_start(
                    out=rows_s[RK_RD, 0, row_i, ch * 128:(ch + 1) * 128],
                    in_=den_r[0:1, :]), inc="srow", n=16)
                b.wt("G", "srow")
                b.em("G", lambda e, row_i=row_i, ch=ch: e.dma_start(
                    out=bcRD[:, :],
                    in_=rows_s[RK_RD, 0, row_i, ch * 128:(ch + 1) * 128]
                        .partition_broadcast(128)), inc="sbc", n=16)
                b.claim("T", "P1")
                b.wt("T", "sck", sck_t)
                for m in range(c.DC):
                    n_steps = (ch + 1) + c.DC
                    step = 0
                    for sb in range(ch + 1):
                        step += 1
                        b.em("T", lambda e, sb=sb, m=m, at_idx=at_idx, step=step,
                             n_steps=n_steps: e.matmul(
                            out=P1[:, m * 128:(m + 1) * 128],
                            lhsT=vtm[:, sb, m * 128:(m + 1) * 128],
                            rhs=AT[:, at_idx[sb], :],
                            start=(step == 1), stop=(step == n_steps)))
                    for d in range(c.DC):
                        step += 1
                        b.em("T", lambda e, d=d, m=m, ccols=ccols, step=step,
                             n_steps=n_steps: e.matmul(
                            out=P1[:, m * 128:(m + 1) * 128],
                            lhsT=kvb3[:, d, m * 128:(m + 1) * 128],
                            rhs=qf[:, d, ccols],
                            start=(step == 1), stop=(step == n_steps)),
                            inc="T" if (step == n_steps and m == c.DC - 1) else None)
                b.wt("V", "T")
                b.wt("V", "sbc")
                for m in range(c.DC):
                    b.em("V", lambda e, m=m, ch=ch: e.tensor_mul(
                        out=afm[:, m, ch * 128:(ch + 1) * 128],
                        in0=P1[:, m * 128:(m + 1) * 128], in1=bcRD[:, :]), inc="V")
                b.release("P1", "V")
            b.release("kvqbuf", "T")
            for d in range(c.DC):
                Pn = "P0" if d % 2 == 0 else "P1"
                P = P0 if d % 2 == 0 else P1
                b.claim("T", Pn)
                nmm = c.CPS * 2
                i_mm = 0
                for tc in range(c.CPS):
                    for nh in range(2):
                        i_mm += 1
                        b.em("T", lambda e, P=P, d=d, tc=tc, nh=nh, i_mm=i_mm,
                             nmm=nmm: e.matmul(
                            out=P[:, nh * 512:(nh + 1) * 512],
                            lhsT=ktm[:, tc, d * 128:(d + 1) * 128],
                            rhs=vtm[:, tc, nh * 512:(nh + 1) * 512],
                            start=(tc == 0), stop=(tc == c.CPS - 1)),
                            inc="T" if i_mm == nmm else None)
                b.wt("V", "T")
                b.em("V", lambda e, P=P, d=d: e.tensor_add(
                    out=kv[:, d, :], in0=kv[:, d, :], in1=P[:, 0:c.D]), inc="V")
                b.release(Pn, "V")
            b.em("V", lambda e: e.tensor_reduce(
                out=kswork[:], in_=kf[:], axis=AX.X, op=ALU.add), inc="V")
            b.flush("V")
            b.em("V", lambda e: e.tensor_add(out=ks[:], in0=ks[:],
                                             in1=kswork[:]), inc="V")
            b.flush("V")
            b.wt("S", "V")
            b.em("S", lambda e: e.copy(out=ksb[:], in_=ks[:]), inc="S")
            b.wt("T", "V")
            for qt in range(4):
                ws = wasl[qt % 2]
                wsem = f"swa{qt % 2}"
                b.claim("Y", f"wac{qt % 2}")
                b.em("Y", lambda e, qt=qt, ws=ws: e.dma_start(
                    out=ws[:], in_=wo_s[:, :, qt * 256:(qt + 1) * 256]),
                    inc=wsem, n=16)
                b.wt("T", wsem)
                for ml in range(2):
                    m = qt * 2 + ml
                    Pn = "P2" if m % 2 == 0 else "P3"
                    P = P2 if m % 2 == 0 else P3
                    b.claim("T", Pn)
                    for k in range(c.DC):
                        b.em("T", lambda e, P=P, k=k, ml=ml, ws=ws: e.matmul(
                            out=P[:, 0:c.TS],
                            lhsT=ws[:, k, ml * 128:(ml + 1) * 128],
                            rhs=afm[:, k, :], start=(k == 0),
                            stop=(k == c.DC - 1)),
                            inc="T" if k == c.DC - 1 else None)
                    if ml == 1:
                        b.release(f"wac{qt % 2}", "T")
                    b.wt("V", "T")
                    b.em("V", lambda e, P=P, m=m, slot=slot: e.scalar_tensor_tensor(
                        out=ring[slot][:, m, 1:TSH - 1], in0=P[:, 0:c.TS],
                        scalar=batt["bo"][:, m:m + 1],
                        in1=ring[slot][:, m, 1:TSH - 1], op0=ALU.add, op1=ALU.add),
                        inc="V")
                    b.release(Pn, "V")

        def gslot(seq, si):
            return (seq * c.NSUP + si) % 3

        def load_x(seq, si):
            slot = gslot(seq, si)
            sem = f"sx{slot}"
            b.claim("Y", f"slotfree{slot}")
            if c.NSUP == 1:
                b.em("Y", lambda e, seq=seq, slot=slot: e.dma_start(
                    out=ring[slot][:, :, 1:TSH - 1],
                    in_=x_in[seq, :, :, 0:c.TS]), inc=sem, n=16)
                b.wt("V", sem)
                b.em("V", lambda e, slot=slot: e.memset(ring[slot][:, :, 0:1], 0.0),
                     inc="V")
                b.em("V", lambda e, slot=slot: e.memset(
                    ring[slot][:, :, TSH - 1:TSH], 0.0), inc="V")
            elif si == 0:
                b.em("Y", lambda e, seq=seq, slot=slot: e.dma_start(
                    out=ring[slot][:, :, 1:TSH],
                    in_=x_in[seq, :, :, 0:c.TS + 1]), inc=sem, n=16)
                b.wt("V", sem)
                b.em("V", lambda e, slot=slot: e.memset(ring[slot][:, :, 0:1], 0.0),
                     inc="V")
            elif si == c.NSUP - 1:
                b.em("Y", lambda e, si=si, seq=seq, slot=slot: e.dma_start(
                    out=ring[slot][:, :, 0:TSH - 1],
                    in_=x_in[seq, :, :, si * c.TS - 1:c.TSEQ]), inc=sem, n=16)
                b.wt("V", sem)
                b.em("V", lambda e, slot=slot: e.memset(
                    ring[slot][:, :, TSH - 1:TSH], 0.0), inc="V")
            else:
                b.em("Y", lambda e, seq=seq, si=si, slot=slot: e.dma_start(
                    out=ring[slot][:],
                    in_=x_in[seq, :, :, si * c.TS - 1:(si + 1) * c.TS + 1]),
                    inc=sem, n=16)

        def wait_x(seq, si):
            slot = gslot(seq, si)
            b.wt("T", f"sx{slot}")
            b.wt("V", f"sx{slot}")
            b.wt("T", "V")

        def store_y(seq, si):
            slot = gslot(seq, si)
            b.wt("Y", "V")
            b.em("Y", lambda e, seq=seq, si=si, slot=slot: e.dma_start(
                out=y_out[seq, :, :, si * c.TS:(si + 1) * c.TS],
                in_=ring[slot][:, :, 1:TSH - 1]), inc=f"so{slot}", n=16)
            b.release(f"slotfree{slot}", f"so{slot}")

        def ffn2_front(seq, si):
            slot = gslot(seq, si)
            if si > 0:
                b.em("V", lambda e, slot=slot, sv=(si - 1) % 2: e.tensor_copy(
                    out=ring[slot][:, :, 0:1], in_=sliver[sv][:]), inc="V")
            if si + 1 < c.NSUP:
                nslot = gslot(seq, si + 1)
                b.em("V", lambda e, slot=slot, nslot=nslot: e.tensor_copy(
                    out=ring[slot][:, :, TSH - 1:TSH], in_=ring[nslot][:, :, 1:2]),
                    inc="V")
            ffn_front(seq, si, slot, 1, si == 0, si == c.NSUP - 1)

        for seq in range(c.NSEQ):
            if seq > 0:
                b.wt("V", "T")
                b.em("V", lambda e: e.memset(kv[:], 0.0), inc="V")
                b.em("V", lambda e: e.memset(ks[:], 0.0), inc="V")
                b.em("V", lambda e: e.memset(ksb[:], 0.0), inc="V")
            load_x(seq, 0)
            if c.NSUP > 1:
                load_x(seq, 1)
            for si in range(c.NSUP):
                wait_x(seq, si)
                if stage == 0:
                    store_y(seq, si)
                    if si + 2 < c.NSUP:
                        load_x(seq, si + 2)
                    continue
                if (seq, si, 0) not in front_state:
                    ffn_front(seq, si, gslot(seq, si), 0,
                              si == 0, si == c.NSUP - 1)
                ffn_back(seq, si, gslot(seq, si), 0)
                if stage >= 2:
                    attn(seq, si, gslot(seq, si))
                if stage >= 3 and si > 0:
                    ffn2_front(seq, si - 1)
                slot = gslot(seq, si)
                b.em("V", lambda e, slot=slot, sv=si % 2: e.tensor_copy(
                    out=sliver[sv][:], in_=ring[slot][:, :, TSH - 2:TSH - 1]),
                    inc="V")
                if si + 1 < c.NSUP:
                    ffn_front(seq, si + 1, gslot(seq, si + 1), 0,
                              False, si + 1 == c.NSUP - 1)
                if stage < 3:
                    store_y(seq, si)
                elif si > 0:
                    ffn_back(seq, si - 1, gslot(seq, si - 1), 1)
                    store_y(seq, si - 1)
                if si + 2 < c.NSUP:
                    load_x(seq, si + 2)
            if stage >= 3:
                ffn2_front(seq, c.NSUP - 1)
                ffn_back(seq, c.NSUP - 1, gslot(seq, c.NSUP - 1), 1)
                store_y(seq, c.NSUP - 1)

        for i in range(3):
            b.wt("Y", f"so{i}")

        with nc.Block() as block:
            @block.tensor
            def _(eng):
                for f in b.Q["T"]:
                    f(eng)

            @block.vector
            def _(eng):
                for f in b.Q["V"]:
                    f(eng)

            @block.scalar
            def _(eng):
                for f in b.Q["S"]:
                    f(eng)

            @block.gpsimd
            def _(eng):
                for f in b.Q["G"]:
                    f(eng)

            @block.sync
            def _(eng):
                for f in b.Q["Y"]:
                    f(eng)

    return nc


def _prep_consts():
    idf = np.eye(128, dtype=np.float32)
    idb = np.eye(128, dtype=ml_dtypes.bfloat16)
    mask = np.triu(np.ones((128, 128), dtype=np.float32))
    onesb = np.ones((128, 1), dtype=ml_dtypes.bfloat16)
    return idf, idb, mask, onesb


def _prep_weights(Wq, bq, Wk, bk, Wv, bv, Wo, bo, w1, cw, cb, w2):
    d = {}

    def wt_t(W):
        return np.ascontiguousarray(W.T.reshape(8, 128, -1).transpose(1, 0, 2))

    d["wq_in"], d["wk_in"], d["wv_in"], d["wo_in"] = map(wt_t, (Wq, Wk, Wv, Wo))
    d["w1_in"] = np.ascontiguousarray(w1.T.reshape(8, 128, 4096).transpose(1, 0, 2))
    d["w2_in"] = np.ascontiguousarray(w2.T.reshape(32, 128, 1024).transpose(1, 0, 2))

    def bvec(v):
        return np.ascontiguousarray(v.reshape(8, 128).T)

    d["bq_in"], d["bk_in"], d["bo_in"] = bvec(bq), bvec(bk), bvec(bo)
    d["bv_in"] = np.ascontiguousarray(bv.reshape(1, 1024))
    d["cw_in"] = np.ascontiguousarray(
        cw[:, 0, :].T.reshape(3, 32, 128).transpose(0, 2, 1))
    d["cb_in"] = np.ascontiguousarray(cb.reshape(32, 128).T)
    return d


_CACHED = {}


def kernel(L0_attn_chunk, L0_attn_Wq, L0_attn_bq, L0_attn_Wk, L0_attn_bk,
           L0_attn_Wv, L0_attn_bv, L0_attn_Wo, L0_attn_bo,
           L0_ffn_w1, L0_ffn_conv_w, L0_ffn_conv_b, L0_ffn_w2):
    cfg = Cfg(nseq=2, tseq=4096)
    n_cores = 8

    x = np.asarray(L0_attn_chunk, dtype=np.float32)
    B, T, D = x.shape
    x_fm = np.ascontiguousarray(x.reshape(B, T, 8, 128).transpose(0, 3, 2, 1))

    wd = _prep_weights(
        np.asarray(L0_attn_Wq, np.float32), np.asarray(L0_attn_bq, np.float32),
        np.asarray(L0_attn_Wk, np.float32), np.asarray(L0_attn_bk, np.float32),
        np.asarray(L0_attn_Wv, np.float32), np.asarray(L0_attn_bv, np.float32),
        np.asarray(L0_attn_Wo, np.float32), np.asarray(L0_attn_bo, np.float32),
        np.asarray(L0_ffn_w1, np.float32), np.asarray(L0_ffn_conv_w, np.float32),
        np.asarray(L0_ffn_conv_b, np.float32), np.asarray(L0_ffn_w2, np.float32))
    idf, idb, mask, onesb = _prep_consts()
    wd.update(dict(idf_in=idf, idb_in=idb, mask_in=mask, onesb_in=onesb))

    if "nc" not in _CACHED:
        _CACHED["nc"] = build_program(cfg)
    nc = _CACHED["nc"]

    in_maps = []
    for ci in range(n_cores):
        m = dict(wd)
        m["x_in"] = np.ascontiguousarray(x_fm[ci * 2:(ci + 1) * 2])
        in_maps.append(m)

    res = run_bass_kernel_spmd(nc, in_maps, core_ids=list(range(n_cores)))
    outs = [r["y_out"] for r in res.results]
    y = np.concatenate(outs, axis=0)
    y = np.ascontiguousarray(
        y.transpose(0, 3, 2, 1).reshape(B, T, D).astype(np.float32))
    return y
